# version 14
# baseline (speedup 1.0000x reference)
import math
from contextlib import ExitStack

import numpy as np
import ml_dtypes

import concourse.bass as bass
import concourse.mybir as mybir
from concourse.bass_utils import run_bass_kernel_spmd

F32 = mybir.dt.float32
F16 = mybir.dt.float16
BF16 = mybir.dt.bfloat16
I32 = mybir.dt.int32
U16 = mybir.dt.uint16
AF = mybir.ActivationFunctionType
ALU = mybir.AluOpType
AX = mybir.AxisListType.X
NPBF16 = ml_dtypes.bfloat16

B, S, HID, NH, LAT, R = 2, 4096, 2048, 16, 512, 32
DH, CDIM = 128, 96
FI = 5632
ISD = 1.0 / math.sqrt(DH)
GROUPS = [[0, 1, 2, 3], [4, 5, 6, 7]]
TWO126 = float(2.0 ** 126)
ENGINES = ("t", "v", "a", "g", "y")


class Ctr:
    def __init__(self, nc, name):
        self.h = nc.alloc_semaphore(name=name)
        self.n = 0


def gate(*ctrs):
    return [(c.h, c.n) for c in ctrs]


class Prog:
    def __init__(self, nc):
        self.nc = nc
        self.steps = {e: [] for e in ENGINES}
        self.pe_was_mm = False

    def wait(self, eng, ctr, val=None):
        v = ctr.n if val is None else val
        if v > 0:
            self.steps[eng].append(("w", ctr.h, v))

    def waits(self, eng, g):
        for (h, v) in g:
            if v > 0:
                self.steps[eng].append(("w", h, v))

    def op(self, eng, fn, inc=None, k=1):
        if inc is not None:
            inc.n += k
            self.steps[eng].append(("o", fn, inc.h, k))
        else:
            self.steps[eng].append(("o", fn, None, 0))

    def drain(self, eng):
        self.steps[eng].append(("d",))

    def inc_last(self, eng, ctr, k=1):
        for i in range(len(self.steps[eng]) - 1, -1, -1):
            st = self.steps[eng][i]
            if st[0] == "o":
                assert st[2] is None, "op already carries an inc"
                ctr.n += k
                self.steps[eng][i] = ("o", st[1], ctr.h, k)
                return
        raise AssertionError("no op to attach inc to")

    def mm(self, fn, inc=None, k=1):
        self.op("t", fn, inc, k)
        self.pe_was_mm = True

    def tr(self, fn, inc=None, k=1):
        if self.pe_was_mm:
            self.steps["t"].append(("n",))
            self.pe_was_mm = False
        self.op("t", fn, inc, k)

    def barrier(self, ctrs):
        for e in ENGINES:
            for c in ctrs:
                self.wait(e, c)

    def flush(self, block):
        emap = {"t": block.tensor, "v": block.vector, "a": block.scalar,
                "g": block.gpsimd, "y": block.sync}
        for e in ENGINES:
            steps = self.steps[e]
            if not steps:
                continue

            def body(eng, steps=steps):
                for st in steps:
                    if st[0] == "w":
                        eng.wait_ge(st[1], st[2])
                    elif st[0] == "d":
                        eng.drain()
                    elif st[0] == "n":
                        eng.nop()
                    else:
                        inst = st[1](eng)
                        if st[2] is not None:
                            inst.then_inc(st[2], st[3])
            emap[e](body)
        self.steps = {e: [] for e in ENGINES}
        self.pe_was_mm = False


def build_nc(debug_outputs=False, ablate=frozenset()):
    nc = bass.Bass("TRN2", target_bir_lowering=False,
                   dynamic_dma_scratch_size=2048)
    p = Prog(nc)

    def din(name, shape, dt):
        return nc.dram_tensor(name, list(shape), dt, kind="ExternalInput")

    xb_in = din("xb_in", [S, HID], F32)
    x_own = din("x_own", [1024, HID], F32)
    wqd_t = din("wqd_t", [HID, LAT], F32)
    wkvd_t = din("wkvd_t", [HID, LAT], F32)
    wqup_t = din("wqup_t", [LAT, 4 * CDIM], F32)
    wkup_t = din("wkup_t", [LAT, 4 * CDIM], F32)
    wqrope_t = din("wqrope_t", [LAT, 4 * R], F32)
    wkrope_t = din("wkrope_t", [HID, R], F32)
    wvup_t = din("wvup_t", [LAT, 4 * DH], F32)
    wo_t_own = din("wo_t_own", [4 * DH, HID], F32)
    bo_bc = din("bo_bc", [128, HID], F32)
    cos_t = din("cos_t", [S, 128], F32)
    sin_t = din("sin_t", [S, 128], F32)
    cos_own = din("cos_own", [1024, 128], F32)
    sin_own = din("sin_own", [1024, 128], F32)
    mask_in = din("mask_in", [128, 896], F32)
    ident_in = din("ident_in", [128, 128], BF16)
    ident32_in = din("ident32_in", [128, 128], F32)
    gup_in = din("gup_in", [11 * 128, 256], I32)
    gus_t = din("gus_t", [11 * 128, 16], F16)
    dnp_in = din("dnp_in", [2 * 128, 704], I32)
    dns_t = din("dns_t", [2 * 128, 44], F16)

    out_own = nc.dram_tensor("out_own", [1024, HID], F32, kind="ExternalOutput")

    gu_contrib = nc.dram_tensor("gu_contrib", [11, 128, 2048], BF16)
    dn_contrib = nc.dram_tensor("dn_contrib", [2, 128, 5632], BF16)
    wdeq_gu = nc.dram_tensor("wdeq_gu", [88, 128, 2048], BF16)
    wdeq_dn = nc.dram_tensor("wdeq_dn", [16, 128, 5632], BF16)
    sp_in = nc.dram_tensor("sp_in", [S, HID], F32)
    lat_c = nc.dram_tensor("lat_c", [128, 8192], BF16)
    lat_g = nc.dram_tensor("lat_g", [512, 8192], BF16)
    kr_c = nc.dram_tensor("kr_c", [32, 1024], BF16)
    kr_g = nc.dram_tensor("kr_g", [128, 1024], BF16)
    s_own_d = nc.dram_tensor("s_own_d", [1024, HID], F32)

    dbg = {}
    if debug_outputs:
        dbg["qT"] = nc.dram_tensor("dbg_qT", [128, 4 * S], BF16, kind="ExternalOutput")
        dbg["kT"] = nc.dram_tensor("dbg_kT", [128, 4 * S], BF16, kind="ExternalOutput")
        dbg["v"] = nc.dram_tensor("dbg_v", [128, 32 * 512], BF16, kind="ExternalOutput")
        dbg["ctxT"] = nc.dram_tensor("dbg_ctxT", [128, 4 * S], BF16, kind="ExternalOutput")
        dbg["sown"] = nc.dram_tensor("dbg_sown", [1024, HID], F32, kind="ExternalOutput")
        dbg["wqd"] = nc.dram_tensor("dbg_wqd", [128, 16 * 512], BF16, kind="ExternalOutput")
        dbg["xnT"] = nc.dram_tensor("dbg_xnT", [128, 16 * 512], BF16, kind="ExternalOutput")
        dbg["qlat"] = nc.dram_tensor("dbg_qlat", [128, 4 * 512], BF16, kind="ExternalOutput")
        dbg["xnT0"] = nc.dram_tensor("dbg_xnT0", [128, 16 * 512], BF16, kind="ExternalOutput")
        dbg["xld7"] = nc.dram_tensor("dbg_xld7", [128, HID], F32, kind="ExternalOutput")

    cy = Ctr(nc, "cy")
    c_w = Ctr(nc, "c_w")
    c_pk = [Ctr(nc, "c_pk0"), Ctr(nc, "c_pk1")]
    c_sc = Ctr(nc, "c_sc")
    c_x = Ctr(nc, "c_x")
    c_x2 = [Ctr(nc, "c_x2a"), Ctr(nc, "c_x2b")]
    c_cs = [Ctr(nc, "c_cs0"), Ctr(nc, "c_cs1")]
    c_gs = [Ctr(nc, "c_gs0"), Ctr(nc, "c_gs1")]
    c_ds = [Ctr(nc, "c_ds0"), Ctr(nc, "c_ds1")]
    c_st = Ctr(nc, "c_st")
    c_sp = [Ctr(nc, "c_sp0"), Ctr(nc, "c_sp1")]
    c_lat = Ctr(nc, "c_lat")
    c_kr = Ctr(nc, "c_kr")
    c_lq = [Ctr(nc, "c_lq0"), Ctr(nc, "c_lq1")]
    cg = Ctr(nc, "cg")
    cdbg = Ctr(nc, "cdbg")
    cc = Ctr(nc, "cc")
    ct = Ctr(nc, "ct")
    cv = Ctr(nc, "cv")
    ca = Ctr(nc, "ca")
    allc = [cy, c_w, c_pk[0], c_pk[1], c_sc, c_x, c_x2[0], c_x2[1], c_cs[0], c_cs[1],
            c_gs[0], c_gs[1], c_ds[0], c_ds[1], c_st, c_sp[0], c_sp[1],
            c_lat, c_kr, c_lq[0], c_lq[1], cg, cdbg, ct, cv, ca]

    def sb(ctx, name, shape, dt):
        return ctx.enter_context(nc.sbuf_tensor(name, list(shape), dt))

    with ExitStack() as ctx0:
        ps = [ctx0.enter_context(nc.psum_tensor(f"ps{i}", [128, 512], F32))
              for i in range(6)]
        psT = ctx0.enter_context(nc.psum_tensor("psT", [128, 512], BF16))
        psT2 = ctx0.enter_context(nc.psum_tensor("psT2", [128, 512], BF16))

        ident_sb = sb(ctx0, "ident_sb", [128, 128], BF16)
        bo_sb = sb(ctx0, "bo_sb", [128, HID], F32)
        stat = sb(ctx0, "stat", [128, 64], F32)

        wspec = [("wqd", wqd_t, 16, LAT), ("wkvd", wkvd_t, 16, LAT),
                 ("wqup", wqup_t, 4, 4 * CDIM), ("wkup", wkup_t, 4, 4 * CDIM),
                 ("wqrope", wqrope_t, 4, 4 * R), ("wkrope", wkrope_t, 16, R),
                 ("wvup", wvup_t, 4, 4 * DH)]

        with ExitStack() as ctxAB:
            qT = sb(ctxAB, "qT", [128, 4, S], BF16)
            kT = sb(ctxAB, "kT", [128, 4, S], BF16)
            v_sb = sb(ctxAB, "v_sb", [128, 32, 512], BF16)
            wo_sb = sb(ctxAB, "wo_sb", [128, 4, HID], BF16)

            scrD = sb(ctxAB, "scrD", [128, 4608], F32)
            dq_bits = scrD[:, 0:1024].bitcast(U16)
            dq_sh = scrD[:, 1024:1280].bitcast(U16)
            dq_hi = scrD[:, 1280:1536].bitcast(U16)
            dq_lo = scrD[:, 1536:1792].bitcast(U16)
            dq_pk = [scrD[:, 1792:2048].bitcast(I32),
                     scrD[:, 2048:2304].bitcast(I32)]
            dq_sc = scrD[:, 2304:2312].bitcast(F16)
            dq_s32 = scrD[:, 2320:2336]
            dq_wscl = scrD[:, 2336:3360].bitcast(BF16).rearrange(
                "q (g o) -> q g o", g=16)
            dq_wob = scrD[:, 3360:4384].bitcast(BF16).rearrange(
                "q (g o) -> q g o", g=16)

            ctxWA = ExitStack()
            wsb = {}
            for name, src, nsub, m in wspec:
                wsb[name] = sb(ctxWA, name + "_sb", [128, nsub, m], BF16)
            wsb["wo"] = wo_sb

            with ExitStack() as ctxP0:
                scr0 = sb(ctxP0, "scr0", [128, 8192], F32)
                wf32 = scr0[:, 0:8192]

                for dst, src in ((ident_sb, ident_in),
                                 (bo_sb, bo_bc)):
                    p.op("y", lambda e, d=dst, s_=src:
                         e.dma_start(out=d[:, :], in_=s_[:, :]), inc=c_w, k=16)

                p.op("v", lambda e: e.memset(stat[:, 12:13], 1e-5))
                p.op("v", lambda e: e.memset(stat[:, 13:14], 1e-6))
                p.drain("v")

                cg_gate = []
                for name, src, nsub, m in (wspec + [("wo", wo_t_own, 4, HID)]):
                    dst = wsb[name]
                    ap_in = src.ap().rearrange("(s q) m -> q s m", q=128)
                    p.waits("y", cg_gate)
                    p.op("y", lambda e, a=ap_in, n=nsub, mm_=m:
                         e.dma_start(out=wf32[:, 0:n * mm_].rearrange(
                             "q (s m) -> q s m", s=n), in_=a), inc=c_w, k=16)
                    p.wait("v", c_w)
                    p.op("v", lambda e, d=dst, n=nsub, mm_=m:
                         e.tensor_copy(d[:, :, :], wf32[:, 0:n * mm_].rearrange(
                             "q (s m) -> q s m", s=n)), inc=cv, k=1)
                    p.drain("v")
                    cg_gate = gate(cv)

                p.barrier(allc)
                with nc.named_scope("P0"), nc.Block() as blk:
                    p.flush(blk)

            with ExitStack() as ctxA:
                cosb = sb(ctxA, "cosb", [128, 2, 128], F32)
                sinb = sb(ctxA, "sinb", [128, 2, 128], F32)
                rsc = sb(ctxA, "rsc", [128, 128], F32)
                rsw = sb(ctxA, "rsw", [128, 128], F32)
                ropebf = sb(ctxA, "ropebf", [128, 128], BF16)

                rsc_gate = []
                ropebf_gate = []
                cs_gate = [[], []]
                psg = {i: [] for i in range(6)}
                psTg = []
                psT2g = []

                with ExitStack() as ctxOwn:
                    xld = sb(ctxOwn, "xld0", [128, HID], F32)
                    xnbf = sb(ctxOwn, "xnbf", [128, HID], BF16)
                    xnh = sb(ctxOwn, "xnh", [128, 16, 512], BF16)
                    latst = sb(ctxOwn, "latst", [128, 4, 512], BF16)
                    krT_sb = sb(ctxOwn, "krT_sb", [32, 8, 128], BF16)

                    xld_gate = []
                    xn_gate = []
                    xnh_war = []
                    latst_gate = []
                    for half in range(2):
                        for tt in range(4):
                            gt = half * 4 + tt
                            p.waits("y", xld_gate)
                            p.op("y", lambda e, g=gt:
                                 e.dma_start(out=xld[:, :],
                                             in_=x_own[g * 128:(g + 1) * 128, :]),
                                 inc=c_x, k=16)
                            p.wait("v", c_x)
                            for sg4 in range(4):
                                p.op("v", lambda e, sg4=sg4: e.bn_stats(
                                    stat[:, 16 + 6 * sg4:22 + 6 * sg4],
                                    xld[:, 512 * sg4:512 * (sg4 + 1)]))
                                p.drain("v")
                            p.op("v", lambda e: e.bn_aggr(
                                stat[:, 0:2], stat[:, 16:40].rearrange(
                                    "q (g s) -> q g s", g=4)))
                            p.drain("v")
                            p.op("v", lambda e: e.tensor_scalar(
                                stat[:, 2:3], stat[:, 0:1], -1.0, None,
                                ALU.mult), inc=cv, k=1)
                            p.wait("a", cv)
                            p.op("a", lambda e: e.activation(
                                stat[:, 5:6], stat[:, 1:2], AF.Sqrt,
                                bias=stat[:, 12:13], scale=1.0), inc=ca, k=1)
                            p.wait("v", ca)
                            p.op("v", lambda e: e.reciprocal(
                                stat[:, 6:7], stat[:, 5:6]))
                            p.drain("v")
                            p.waits("v", xn_gate)
                            p.op("v", lambda e: e.tensor_scalar(
                                xnbf[:, :], xld[:, :], stat[:, 2:3],
                                stat[:, 6:7], ALU.add, ALU.mult), inc=cv, k=1)
                            xld_gate = gate(cv)
                            p.wait("t", cv)
                            for b4 in range(4):
                                pst = psT if b4 % 2 == 0 else psT2
                                if b4 % 2 == 0:
                                    p.waits("t", psTg)
                                else:
                                    p.waits("t", psT2g)
                                for i in range(b4 * 4, b4 * 4 + 4):
                                    sl = (i % 4) * 128
                                    p.tr(lambda e, i=i, sl=sl, pst=pst:
                                         e.transpose(
                                             pst[0:128, sl:sl + 128],
                                             xnbf[:, i * 128:(i + 1) * 128],
                                             ident_sb[:, :]))
                                p.inc_last("t", ct)
                                p.wait("a", ct)
                                if tt == 0 and b4 == 0:
                                    p.waits("a", xnh_war)
                                for i in range(b4 * 4, b4 * 4 + 4):
                                    sl = (i % 4) * 128
                                    p.op("a", lambda e, i=i, sl=sl, t=tt,
                                         pst=pst:
                                         e.copy(xnh[:, i, t * 128:(t + 1) * 128],
                                                pst[0:128, sl:sl + 128]))
                                p.op("a", lambda e: e.drain(), inc=ca, k=1)
                                if b4 % 2 == 0:
                                    psTg = gate(ca)
                                else:
                                    psT2g = gate(ca)
                            xn_gate = gate(ca)
                        xnh_gate = gate(ca)

                        p.waits("t", xnh_gate)
                        for wi, wname in ((0, "wqd"), (1, "wkvd")):
                            for lt in range(4):
                                pi = lt % 2
                                p.waits("t", psg[pi])
                                for su in range(16):
                                    p.mm(lambda e, w=wsb[wname], su=su, lt=lt,
                                         pi=pi: e.matmul(
                                             ps[pi][:, :],
                                             w[:, su, lt * 128:(lt + 1) * 128],
                                             xnh[:, su, :], start=(su == 0),
                                             stop=(su == 15)))
                                p.inc_last("t", ct)
                                p.wait("a", ct)
                                if lt == 0:
                                    p.waits("a", latst_gate)
                                p.op("a", lambda e, lt=lt, pi=pi: e.copy(
                                    latst[:, lt, :], ps[pi][:, :]), inc=ca, k=1)
                                psg[pi] = gate(ca)
                            p.wait("a", ca)
                            p.op("a", lambda e, wi=wi, half=half: e.dma_start(
                                out=lat_c.ap()[:, wi * 4096:(wi + 1) * 4096]
                                .rearrange("p (lt t) -> p lt t", lt=4)
                                [:, :, half * 512:(half + 1) * 512],
                                in_=latst[:, :, :]), inc=c_lat, k=16)
                            latst_gate = gate(c_lat)

                        for tt in range(4):
                            t8 = half * 4 + tt
                            cslot = t8 % 2
                            p.waits("y", cs_gate[cslot])
                            p.op("y", lambda e, g=t8, t=cslot:
                                 e.dma_start(out=cosb[:, t, :],
                                             in_=cos_own[g * 128:(g + 1) * 128, :]),
                                 inc=c_cs[cslot], k=16)
                            p.op("y", lambda e, g=t8, t=cslot:
                                 e.dma_start(out=sinb[:, t, :],
                                             in_=sin_own[g * 128:(g + 1) * 128, :]),
                                 inc=c_cs[cslot], k=16)
                            pi = 5
                            p.waits("t", psg[pi])
                            for su in range(16):
                                p.mm(lambda e, su=su, t=tt, pi=pi:
                                     e.matmul(
                                         ps[pi][:, 0:R],
                                         xnh[:, su, t * 128:(t + 1) * 128],
                                         wsb["wkrope"][:, su, :],
                                         start=(su == 0), stop=(su == 15)))
                            p.inc_last("t", ct)
                            p.wait("a", ct)
                            p.waits("a", rsc_gate)
                            p.op("a", lambda e, pi=pi: e.copy(
                                rsc[:, 0:R], ps[pi][:, 0:R]), inc=ca, k=1)
                            psg[pi] = gate(ca)
                            p.wait("v", ca)
                            p.wait("v", c_cs[cslot])
                            p.op("v", lambda e: e.tensor_copy(
                                rsw[:, 0:16], rsc[:, 16:32]))
                            p.op("v", lambda e: e.tensor_copy(
                                rsw[:, 16:32], rsc[:, 0:16]))
                            p.drain("v")
                            p.op("v", lambda e, t=cslot: e.tensor_tensor(
                                rsc[:, 0:R], rsc[:, 0:R], cosb[:, t, 0:R],
                                ALU.mult))
                            p.op("v", lambda e, t=cslot: e.tensor_tensor(
                                rsw[:, 0:R], rsw[:, 0:R], sinb[:, t, 0:R],
                                ALU.mult))
                            p.drain("v")
                            p.waits("v", ropebf_gate)
                            p.op("v", lambda e: e.tensor_tensor(
                                ropebf[:, 0:R], rsc[:, 0:R], rsw[:, 0:R],
                                ALU.add), inc=cv, k=1)
                            rsc_gate = gate(cv)
                            cs_gate[cslot] = gate(cv)
                            p.wait("t", cv)
                            p.waits("t", psT2g)
                            p.tr(lambda e: e.transpose(
                                psT2[0:R, 0:128], ropebf[:, 0:R],
                                ident_sb[:, :]), inc=ct, k=1)
                            ropebf_gate = gate(ct)
                            p.wait("a", ct)
                            p.op("a", lambda e, g=t8: e.copy(
                                krT_sb[:, g, :], psT2[0:R, 0:128]))
                            p.op("a", lambda e: e.drain(), inc=ca, k=1)
                            psT2g = gate(ca)
                        xnh_war = gate(ct)
                    p.wait("a", ca)
                    p.op("a", lambda e: e.dma_start(
                        out=kr_c.ap()[:, :],
                        in_=krT_sb[:, :, :].rearrange("p a b -> p (a b)")),
                        inc=c_lat, k=16)

                    p.wait("g", c_lat)
                    p.op("g", lambda e: e.collective_compute(
                        "AllGather", ALU.bypass, replica_groups=GROUPS,
                        ins=[lat_c.ap().opt()], outs=[lat_g.ap().opt()]),
                        inc=cc, k=1)
                    p.op("g", lambda e: e.collective_compute(
                        "AllGather", ALU.bypass, replica_groups=GROUPS,
                        ins=[kr_c.ap().opt()], outs=[kr_g.ap().opt()]),
                        inc=cc, k=1)

                p.wait("y", cc, 2)
                for c4 in range(4):
                    for h in range(4):
                        p.op("y", lambda e, c4=c4, h=h: e.dma_start(
                            out=kT[CDIM:128, h, c4 * 1024:(c4 + 1) * 1024],
                            in_=kr_g.ap()[c4 * 32:(c4 + 1) * 32, :]),
                            inc=c_kr, k=16)


                def dequant_gen():
                    pk_gate = [[], []]
                    sc_gate = []
                    dec_gate = []
                    wob_gate = []
                    units = [("gu", ot, 0) for ot in range(11)] +                             [("dn", ot, qq) for ot in range(2)
                             for qq in range(4)]
                    for ui, (kind, ot, qq) in enumerate(units):
                        Wu = 256 if kind == "gu" else 176
                        U = 2 * Wu
                        Gd = Wu // 16
                        srcp = gup_in if kind == "gu" else dnp_in
                        ssrc = gus_t if kind == "gu" else dns_t
                        ddst = gu_contrib if kind == "gu" else dn_contrib
                        buf = dq_pk[ui % 2]
                        p.waits("y", pk_gate[ui % 2])
                        p.op("y", lambda e, b=buf, s_=srcp, o=ot, q=qq, w=Wu:
                             e.dma_start(out=b[:, 0:w],
                                         in_=s_[o * 128:(o + 1) * 128,
                                                q * w:(q + 1) * w]),
                             inc=c_pk[ui % 2], k=16)
                        p.waits("y", sc_gate)
                        p.op("y", lambda e, s_=ssrc, o=ot, q=qq, g=Gd:
                             e.dma_start(out=dq_sc[:, 0:g],
                                         in_=s_[o * 128:(o + 1) * 128,
                                                q * g:(q + 1) * g]),
                             inc=c_sc, k=16)
                        yield
                        p.wait("v", c_pk[ui % 2])
                        p.wait("v", c_sc)
                        p.waits("v", dec_gate)
                        p.op("v", lambda e, g=Gd: e.tensor_scalar(
                            dq_s32[:, 0:g], dq_sc[:, 0:g], TWO126, None,
                            ALU.mult))
                        u16v = buf[:, 0:Wu].bitcast(U16)
                        for j in range(4):
                            sv = u16v
                            if j > 0:
                                p.drain("v")
                                p.op("v", lambda e, u=u16v, jj=j, UU=U:
                                     e.tensor_scalar(
                                         dq_sh[:, 0:UU], u, 4 * jj, None,
                                         ALU.logical_shift_right))
                                p.drain("v")
                                sv = dq_sh[:, 0:U]
                            p.op("v", lambda e, s_=sv, UU=U: e.tensor_scalar(
                                dq_hi[:, 0:UU], s_, 8, 12, ALU.bitwise_and,
                                ALU.logical_shift_left))
                            p.op("v", lambda e, s_=sv, UU=U: e.tensor_scalar(
                                dq_lo[:, 0:UU], s_, 7, 6, ALU.bitwise_and,
                                ALU.logical_shift_left))
                            p.drain("v")
                            p.op("v", lambda e, jj=j, UU=U: e.tensor_tensor(
                                dq_bits[:, 0:4 * UU].rearrange(
                                    "q (u j) -> q u j", j=4)[:, :, jj],
                                dq_hi[:, 0:UU], dq_lo[:, 0:UU],
                                ALU.bitwise_or))
                            yield
                        p.drain("v")
                        bview = dq_bits[:, 0:4 * U].bitcast(BF16)
                        for g in range(Gd):
                            p.op("v", lambda e, g=g, bv=bview: e.tensor_scalar(
                                dq_wscl[:, g, :], bv[:, 128 * g:128 * (g + 1)],
                                dq_s32[:, g:g + 1], None, ALU.mult))
                            if g % 6 == 5:
                                yield
                        p.drain("v")
                        p.op("v", lambda e: e.engine_nop(), inc=cv, k=1)
                        wsc_cv = cv.n
                        pk_gate[ui % 2] = gate(cv)
                        sc_gate = gate(cv)
                        yield
                        p.wait("t", cv, wsc_cv)
                        for g0 in range(0, Gd, 4):
                            gges = list(range(g0, min(g0 + 4, Gd)))
                            p.wait("t", ca)
                            for g in gges:
                                sl = (g % 4) * 128
                                p.tr(lambda e, g=g, sl=sl: e.transpose(
                                    psT[0:128, sl:sl + 128], dq_wscl[:, g, :],
                                    ident_sb[:, :]))
                            p.inc_last("t", ct)
                            p.wait("a", ct)
                            if g0 == 0:
                                p.waits("a", wob_gate)
                            for g in gges:
                                sl = (g % 4) * 128
                                p.op("a", lambda e, g=g, sl=sl: e.copy(
                                    dq_wob[:, g, :], psT[0:128, sl:sl + 128]))
                            p.op("a", lambda e: e.drain(), inc=ca, k=1)
                            yield
                        dec_gate = gate(ct)
                        p.wait("a", ca)
                        p.op("a", lambda e, d=ddst, o=ot, q=qq, g=Gd:
                             e.dma_start(
                                 out=d.ap()[o, :,
                                            q * g * 128:(q + 1) * g * 128],
                                 in_=dq_wob[:, 0:g, :].rearrange(
                                     "q g o -> q (g o)")),
                             inc=cg, k=16)
                        wob_gate = gate(cg)
                        yield

                dq_gen = dequant_gen()
                dq_live = [True]

                def dq_step():
                    if dq_live[0]:
                        try:
                            next(dq_gen)
                        except StopIteration:
                            dq_live[0] = False

                for _ in range(3):
                    dq_step()

                qlatb = [sb(ctxA, f"qlatb{i}", [128, 4, 512], BF16)
                         for i in range(2)]
                kvlatb = [sb(ctxA, f"kvlatb{i}", [128, 4, 512], BF16)
                          for i in range(2)]
                lq_gate = [[], []]
                for ck in range(8):
                    bi = ck % 2
                    c4 = ck // 2
                    toff = (ck % 2) * 512
                    p.waits("y", lq_gate[bi])
                    p.wait("y", cc, 1)
                    p.op("y", lambda e, b=qlatb[bi], c4=c4, to=toff:
                         e.dma_start(out=b[:, :, :],
                                     in_=lat_g.ap()[c4 * 128:(c4 + 1) * 128,
                                                    0:4096]
                                     .rearrange("p (lt t) -> p lt t", lt=4)
                                     [:, :, to:to + 512]), inc=c_lq[bi], k=16)
                    p.op("y", lambda e, b=kvlatb[bi], c4=c4, to=toff:
                         e.dma_start(out=b[:, :, :],
                                     in_=lat_g.ap()[c4 * 128:(c4 + 1) * 128,
                                                    4096:8192]
                                     .rearrange("p (lt t) -> p lt t", lt=4)
                                     [:, :, to:to + 512]), inc=c_lq[bi], k=16)

                    p.wait("t", c_lq[bi])
                    for (wname, latb, dstT) in (("wkup", kvlatb[bi], kT),
                                                ("wqup", qlatb[bi], qT)):
                        for h in range(4):
                            pi = 2 + (h % 2)
                            p.waits("t", psg[pi])
                            for ls in range(4):
                                p.mm(lambda e, w=wsb[wname], ls=ls, h=h,
                                     la=latb, pi=pi: e.matmul(
                                         ps[pi][0:CDIM, :],
                                         w[:, ls, h * CDIM:(h + 1) * CDIM],
                                         la[:, ls, :], start=(ls == 0),
                                         stop=(ls == 3)))
                            p.inc_last("t", ct)
                            p.wait("a", ct)
                            p.op("a", lambda e, d=dstT, h=h, c=ck, pi=pi:
                                 e.copy(d[0:CDIM, h, c * 512:(c + 1) * 512],
                                        ps[pi][0:CDIM, :]), inc=ca, k=1)
                            psg[pi] = gate(ca)
                            if h % 2 == 1:
                                dq_step()

                    for tt in range(4):
                        gt = ck * 4 + tt
                        pi = 4
                        p.waits("t", psg[pi])
                        for ls in range(4):
                            p.mm(lambda e, ls=ls, t=tt, pi=pi, la=kvlatb[bi]:
                                 e.matmul(
                                     ps[pi][:, :],
                                     la[:, ls, t * 128:(t + 1) * 128],
                                     wsb["wvup"][:, ls, :], start=(ls == 0),
                                     stop=(ls == 3)))
                        p.inc_last("t", ct)
                        p.wait("v", ct)
                        p.op("v", lambda e, g=gt, pi=pi: e.tensor_copy(
                            v_sb[:, g, :], ps[pi][:, :]), inc=cv, k=1)
                        p.drain("v")
                        psg[pi] = gate(cv)
                        if tt % 2 == 1:
                            dq_step()

                    for tt in range(4):
                        gt = ck * 4 + tt
                        cslot = gt % 2
                        p.waits("y", cs_gate[cslot])
                        p.op("y", lambda e, g=gt, t=cslot:
                             e.dma_start(out=cosb[:, t, :],
                                         in_=cos_t[g * 128:(g + 1) * 128, :]),
                             inc=c_cs[cslot], k=16)
                        p.op("y", lambda e, g=gt, t=cslot:
                             e.dma_start(out=sinb[:, t, :],
                                         in_=sin_t[g * 128:(g + 1) * 128, :]),
                             inc=c_cs[cslot], k=16)
                        pi = 5
                        p.waits("t", psg[pi])
                        for ls in range(4):
                            p.mm(lambda e, ls=ls, t=tt, pi=pi, la=qlatb[bi]:
                                 e.matmul(
                                     ps[pi][:, 0:128],
                                     la[:, ls, t * 128:(t + 1) * 128],
                                     wsb["wqrope"][:, ls, :], start=(ls == 0),
                                     stop=(ls == 3)))
                        p.inc_last("t", ct)
                        p.wait("a", ct)
                        p.waits("a", rsc_gate)
                        p.op("a", lambda e, pi=pi: e.copy(
                            rsc[:, :], ps[pi][:, 0:128]), inc=ca, k=1)
                        psg[pi] = gate(ca)
                        p.wait("v", ca)
                        p.wait("v", c_cs[cslot])
                        p.op("v", lambda e: e.tensor_copy(
                            rsw[:, :].rearrange("q (h r) -> q h r", h=4)[:, :, 0:16],
                            rsc[:, :].rearrange("q (h r) -> q h r", h=4)[:, :, 16:32]))
                        p.op("v", lambda e: e.tensor_copy(
                            rsw[:, :].rearrange("q (h r) -> q h r", h=4)[:, :, 16:32],
                            rsc[:, :].rearrange("q (h r) -> q h r", h=4)[:, :, 0:16]))
                        p.drain("v")
                        p.op("v", lambda e, t=cslot: e.tensor_tensor(
                            rsc[:, :], rsc[:, :], cosb[:, t, :], ALU.mult))
                        p.op("v", lambda e, t=cslot: e.tensor_tensor(
                            rsw[:, :], rsw[:, :], sinb[:, t, :], ALU.mult))
                        p.drain("v")
                        p.waits("v", ropebf_gate)
                        p.op("v", lambda e: e.tensor_tensor(
                            ropebf[:, :], rsc[:, :], rsw[:, :], ALU.add),
                            inc=cv, k=1)
                        rsc_gate = gate(cv)
                        cs_gate[cslot] = gate(cv)
                        p.wait("t", cv)
                        p.waits("t", psT2g)
                        for h in range(4):
                            p.tr(lambda e, h=h: e.transpose(
                                psT2[0:R, h * 128:(h + 1) * 128],
                                ropebf[:, h * R:(h + 1) * R], ident_sb[:, :]))
                        p.inc_last("t", ct)
                        ropebf_gate = gate(ct)
                        p.wait("a", ct)
                        for h in range(4):
                            p.op("a", lambda e, h=h, g=gt: e.copy(
                                qT[CDIM:128, h, g * 128:(g + 1) * 128],
                                psT2[0:R, h * 128:(h + 1) * 128]))
                        p.op("a", lambda e: e.drain(), inc=ca, k=1)
                        psT2g = gate(ca)
                    lq_gate[bi] = gate(ct)
                p.wait("t", c_kr)

                if debug_outputs:
                    p.barrier(allc)
                    p.op("a", lambda e: e.dma_start(
                        out=dbg["wqd"][:, :],
                        in_=wsb["wqd"][:, :, :].rearrange("q h s -> q (h s)")),
                        inc=cdbg, k=16)
                    p.op("a", lambda e: e.dma_start(
                        out=dbg["qT"][:, :],
                        in_=qT[:, :, :].rearrange("q h s -> q (h s)")),
                        inc=cdbg, k=16)
                    p.op("a", lambda e: e.dma_start(
                        out=dbg["kT"][:, :],
                        in_=kT[:, :, :].rearrange("q h s -> q (h s)")),
                        inc=cdbg, k=16)
                    p.op("a", lambda e: e.dma_start(
                        out=dbg["v"][:, :],
                        in_=v_sb[:, :, :].rearrange("q h s -> q (h s)")),
                        inc=cdbg, k=16)

                p.barrier(allc)
                with nc.named_scope("A"), nc.Block() as blk:
                    p.flush(blk)

            ctxWA.close()

            with ExitStack() as ctxB:
                mask_sb = sb(ctxB, "mask_sb", [128, 896], F32)
                ctxT = sb(ctxB, "ctxT", [128, 4, S], BF16)
                p.op("y", lambda e: e.dma_start(out=mask_sb[:, :],
                                                in_=mask_in[:, :]),
                     inc=cy, k=16)
                p.wait("v", cy)
                pt_sb = sb(ctxB, "pt_sb", [128, 3, 512], BF16)
                sqb = sb(ctxB, "sqb", [128, 2, 512], BF16)
                nrm = sb(ctxB, "nrm", [1, 64], F32)
                onec = sb(ctxB, "onec", [128, 1], BF16)
                oner = sb(ctxB, "oner", [1, 128], F32)
                biasc = sb(ctxB, "biasc", [128, 4], F32)
                linvr = sb(ctxB, "linvr", [1, 512], F32)
                linvb = sb(ctxB, "linvb", [128, 512], F32)
                p.op("v", lambda e: e.memset(onec[:, :], 1.0))
                p.op("v", lambda e: e.memset(oner[:, :], 1.0))
                p.drain("v")
                p.op("v", lambda e: e.engine_nop(), inc=cv, k=1)
                ones_gate = gate(cv)


                G = {"psS": [[], []],
                     "pt": [[], [], []],
                     "sq": [[], []],
                     "ps5": [],
                     "l": [],
                     "ctx": [[], []],
                     "nrm": [],
                     "nrm2": []}
                bias_done = [0, 0, 0, 0]
                psc_g = {0: [], 1: [], 2: [], 3: []}
                pending_epi = [None]
                qg_idx = [0]

                def emit_epi(h, qg, cb, par, end_ct):
                    p.wait("v", ct, end_ct)
                    p.op("v", lambda e: e.reciprocal(
                        linvr[0:1, :], ps[3][0:1, :]), inc=cv, k=1)
                    G["l"] = gate(cv)
                    p.wait("t", cv)
                    p.waits("t", G["ps5"])
                    p.mm(lambda e: e.matmul(
                        ps[5][:, :], oner[:, :], linvr[0:1, :],
                        start=True, stop=True), inc=ct, k=1)
                    p.wait("a", ct)
                    p.op("a", lambda e: e.copy(linvb[:, :], ps[5][:, :]),
                         inc=ca, k=1)
                    G["ps5"] = gate(ca)
                    p.wait("v", ca)
                    p.op("v", lambda e, h=h, qg=qg, cb=cb: e.tensor_tensor(
                        ctxT[:, h, qg * 512:(qg + 1) * 512],
                        cb[:, :], linvb[:, :], ALU.mult), inc=cv, k=1)
                    G["ctx"][par] = gate(cv)

                for h in range(4):
                    if pending_epi[0] is not None:
                        pending_epi[0]()
                        pending_epi[0] = None
                    rd_done = {}
                    for idx in range(16):
                        src_i, t8 = idx // 8, idx % 8
                        srcT = qT if src_i == 0 else kT
                        slot = idx % 2
                        row = 64 if (idx % 2) else 0
                        p.waits("v", G["sq"][slot])
                        p.waits("v", G["nrm"])
                        p.op("v", lambda e, h=h, s=srcT, t8=t8, slot=slot:
                             e.tensor_tensor(
                                 sqb[:, slot, :],
                                 s[:, h, t8 * 512:(t8 + 1) * 512],
                                 s[:, h, t8 * 512:(t8 + 1) * 512], ALU.mult),
                             inc=cv, k=1)
                        p.wait("t", cv)
                        if idx == 0:
                            p.waits("t", G["ps5"])
                            if h == 0:
                                p.waits("t", ones_gate)
                        if idx >= 2:
                            p.wait("t", cv, rd_done[idx - 2])
                        p.mm(lambda e, slot=slot, row=row: e.matmul(
                            ps[5][row:row + 1, :], onec[:, :],
                            sqb[:, slot, :],
                            start=True, stop=True), inc=ct, k=1)
                        G["sq"][slot] = gate(ct)
                        p.wait("v", ct)
                        p.op("v", lambda e, idx=idx, row=row: e.reduce_max(
                            nrm[0:1, idx:idx + 1],
                            ps[5][row:row + 1, :], AX), inc=cv, k=1)
                        rd_done[idx] = cv.n
                    G["ps5"] = gate(cv)
                    p.waits("v", G["nrm2"])
                    p.op("v", lambda e: e.reduce_max(
                        nrm[0:1, 16:17], nrm[0:1, 0:8], AX))
                    p.drain("v")
                    p.op("v", lambda e: e.reduce_max(
                        nrm[0:1, 17:18], nrm[0:1, 8:16], AX), inc=cv, k=1)
                    p.wait("a", cv)
                    p.op("a", lambda e: e.activation(
                        nrm[0:1, 20:22], nrm[0:1, 16:18], AF.Sqrt),
                        inc=ca, k=1)
                    G["nrm"] = gate(ca)
                    p.wait("v", ca)
                    p.op("v", lambda e: e.tensor_tensor(
                        nrm[0:1, 24:25], nrm[0:1, 20:21], nrm[0:1, 21:22],
                        ALU.mult))
                    p.drain("v")
                    p.op("v", lambda e: e.tensor_scalar(
                        nrm[0:1, 25:26], nrm[0:1, 24:25], -ISD, -2.0,
                        ALU.mult, ALU.add), inc=cv, k=1)
                    p.wait("t", cv)
                    p.waits("t", G["ps5"])
                    p.mm(lambda e: e.matmul(ps[5][0:128, 0:1], oner[:, :],
                                            nrm[0:1, 25:26],
                                            start=True, stop=True), inc=ct, k=1)
                    G["nrm2"] = gate(ct)
                    p.wait("a", ct)
                    p.op("a", lambda e, h=h: e.copy(
                        biasc[:, h:h + 1], ps[5][:, 0:1]), inc=ca, k=1)
                    bias_done[h] = ca.n
                    G["ps5"] = gate(ca)

                    for qg in range(8):
                        nkt = 4 * qg + 4
                        par = qg_idx[0] % 2
                        cb = ps[4] if par == 0 else ps[2]
                        s_done = {}
                        m_done = {}
                        exp_done = {}

                        def emit_s(kt, qg=qg, h=h, s_done=s_done,
                                   m_done=m_done):
                            i = kt % 2
                            p.waits("t", G["psS"][i])
                            p.mm(lambda e, h=h, qg=qg, kt=kt, i=i: e.matmul(
                                ps[i][:, :], kT[:, h, kt * 128:(kt + 1) * 128],
                                qT[:, h, qg * 512:(qg + 1) * 512],
                                start=True, stop=True), inc=ct, k=1)
                            s_done[kt] = ct.n
                            if kt >= 4 * qg:
                                off = 384 - 128 * (kt - 4 * qg)
                                p.wait("v", ct, s_done[kt])
                                p.op("v", lambda e, i=i, off=off:
                                     e.tensor_tensor(
                                         ps[i][:, :], ps[i][:, :],
                                         mask_sb[:, off:off + 512], ALU.add),
                                     inc=cv, k=1)
                                m_done[kt] = cv.n

                        def emit_exp(kt, h=h, s_done=s_done, m_done=m_done,
                                     exp_done=exp_done):
                            i = kt % 2
                            j = kt % 3
                            if kt in m_done:
                                p.wait("a", cv, m_done[kt])
                            else:
                                p.wait("a", ct, s_done[kt])
                            p.wait("a", ca, bias_done[h])
                            p.waits("a", G["pt"][j])
                            p.op("a", lambda e, h=h, i=i, j=j: e.activation(
                                pt_sb[:, j, :], ps[i][:, :], AF.Exp,
                                bias=biasc[:, h:h + 1], scale=ISD),
                                inc=ca, k=1)
                            exp_done[kt] = ca.n
                            G["psS"][i] = gate(ca)

                        def emit_lctx(kt, h=h, nkt=nkt, cb=cb, par=par,
                                      exp_done=exp_done):
                            j = kt % 3
                            p.wait("t", ca, exp_done[kt])
                            if kt == 0:
                                p.waits("t", G["l"])
                                p.waits("t", G["ctx"][par])
                            p.mm(lambda e, kt=kt, j=j, n=nkt: e.matmul(
                                ps[3][0:1, :], onec[:, :], pt_sb[:, j, :],
                                start=(kt == 0), stop=(kt == n - 1)))
                            p.mm(lambda e, h=h, kt=kt, j=j, n=nkt, cb=cb:
                                 e.matmul(
                                     cb[:, :],
                                     v_sb[:, kt, h * 128:(h + 1) * 128],
                                     pt_sb[:, j, :],
                                     start=(kt == 0), stop=(kt == n - 1)),
                                 inc=ct, k=1)
                            G["pt"][j] = gate(ct)

                        emit_s(0)
                        emit_exp(0)
                        emit_s(1)
                        emit_exp(1)
                        if pending_epi[0] is not None:
                            pending_epi[0]()
                        for kt in range(2, nkt):
                            emit_s(kt)
                            emit_exp(kt)
                            emit_lctx(kt - 2)
                            if kt < 4 * qg:
                                dq_step()
                        emit_lctx(nkt - 2)
                        emit_lctx(nkt - 1)
                        pending_epi[0] = (
                            lambda h=h, qg=qg, cb=cb, par=par, end=ct.n:
                            emit_epi(h, qg, cb, par, end))
                        qg_idx[0] += 1

                pending_epi[0]()
                pending_epi[0] = None
                while dq_live[0]:
                    dq_step()
                p.wait("g", cg)
                p.op("g", lambda e: e.collective_compute(
                    "AllGather", ALU.bypass, replica_groups=[list(range(8))],
                    ins=[gu_contrib.ap().opt()], outs=[wdeq_gu.ap().opt()]),
                    inc=cc, k=1)
                p.op("g", lambda e: e.collective_compute(
                    "AllGather", ALU.bypass, replica_groups=[list(range(8))],
                    ins=[dn_contrib.ap().opt()], outs=[wdeq_dn.ap().opt()]),
                    inc=cc, k=1)
                spbuf = [sb(ctxB, f"spbuf{i}", [128, HID], F32)
                         for i in range(2)]
                p.wait("t", ca)
                p.wait("t", cv)

                if debug_outputs:
                    p.barrier(allc)
                    p.op("a", lambda e: e.dma_start(
                        out=dbg["ctxT"][:, :],
                        in_=ctxT[:, :, :].rearrange("q h s -> q (h s)")),
                        inc=cdbg, k=16)

                sp_g = [gate(cg), gate(cg)]
                for qt32 in range(32):
                    sbi = qt32 % 2
                    for oc in range(4):
                        pi = oc % 2
                        p.waits("t", psc_g[pi])
                        for ds in range(4):
                            p.mm(lambda e, ds=ds, qt=qt32, oc=oc, pi=pi:
                                 e.matmul(ps[pi][:, :],
                                          ctxT[:, ds, qt * 128:(qt + 1) * 128],
                                          wo_sb[:, ds, oc * 512:(oc + 1) * 512],
                                          start=(ds == 0), stop=(ds == 3)))
                        p.inc_last("t", ct)
                        p.wait("a", ct)
                        if oc == 0:
                            p.waits("a", sp_g[sbi])
                        p.op("a", lambda e, oc=oc, pi=pi, sbi=sbi: e.copy(
                            spbuf[sbi][:, oc * 512:(oc + 1) * 512],
                            ps[pi][:, :]), inc=ca, k=1)
                        psc_g[pi] = gate(ca)
                    p.wait("a", ca)
                    p.op("a", lambda e, qt=qt32, sbi=sbi: e.dma_start(
                        out=sp_in[qt * 128:(qt + 1) * 128, :],
                        in_=spbuf[sbi][:, :]), inc=c_sp[sbi], k=16)
                    sp_g[sbi] = gate(c_sp[sbi])

                p.wait("g", c_sp[0])
                p.wait("g", c_sp[1])
                p.op("g", lambda e: e.collective_compute(
                    "ReduceScatter", ALU.add, replica_groups=GROUPS,
                    ins=[sp_in.ap().opt()], outs=[s_own_d.ap().opt()]),
                    inc=cc, k=1)

                p.barrier(allc)
                with nc.named_scope("BW"), nc.Block() as blk:
                    p.flush(blk)

        with ExitStack() as ctxE:
            ident32_sb = sb(ctxE, "ident32_sb", [128, 128], F32)
            xld2 = [sb(ctxE, f"xld2_{i}", [128, HID], F32) for i in range(2)]
            p.op("y", lambda e: e.dma_start(out=ident32_sb[:, :],
                                            in_=ident32_in[:, :]), inc=cy, k=16)
            p.wait("t", cy)
            sq2 = sb(ctxE, "sq2", [128, HID], F32)
            xnbf2 = sb(ctxE, "xnbf2", [128, HID], BF16)
            xnTo = sb(ctxE, "xnTo", [128, 16, 1024], BF16)
            hT = sb(ctxE, "hT", [128, 44, 512], BF16)
            gslab = [sb(ctxE, f"gslab{i}", [128, 16, 128], BF16) for i in range(2)]
            dslab = [sb(ctxE, f"dslab{i}", [128, 44, 128], BF16) for i in range(2)]
            sg_sb = sb(ctxE, "sg_sb", [128, 512], BF16)
            ytp = sb(ctxE, "ytp", [128, 512], F32)
            ybuf = sb(ctxE, "ybuf", [128, 4, HID], F32)
            ynb = sb(ctxE, "ynb", [128, HID], F32)
            gbuf = sb(ctxE, "gbuf", [128, HID], F32)
            st_sb = sb(ctxE, "st_sb", [128, HID], F32)
            ot_sb = sb(ctxE, "ot_sb", [128, HID], F32)

            xld_gate = [[], []]
            xn_gate = []
            psTg = {0: [], 1: []}
            for gt in range(8):
                bi = gt % 2
                p.waits("y", xld_gate[bi])
                p.op("y", lambda e, b=xld2[bi], g=gt:
                     e.dma_start(out=b[:, :],
                                 in_=x_own[g * 128:(g + 1) * 128, :]),
                     inc=c_x2[bi], k=16)
                p.wait("v", c_x2[bi])
                p.op("v", lambda e, b=xld2[bi]: e.reduce_sum(
                    stat[:, 0:1], b[:, :], AX))
                p.wait("a", c_x2[bi])
                p.waits("a", xn_gate)
                p.op("a", lambda e, b=xld2[bi]: e.activation(
                    sq2[:, :], b[:, :], AF.Square, accum_out=stat[:, 7:8]),
                    inc=ca, k=1)
                p.drain("v")
                p.op("v", lambda e: e.tensor_scalar(
                    stat[:, 1:2], stat[:, 0:1], -1.0 / HID, None, ALU.mult))
                p.wait("v", ca)
                p.op("v", lambda e: e.tensor_scalar(
                    stat[:, 2:3], stat[:, 7:8], 1.0 / HID, None, ALU.mult))
                p.drain("v")
                p.op("v", lambda e: e.tensor_tensor(
                    stat[:, 3:4], stat[:, 1:2], stat[:, 1:2], ALU.mult))
                p.drain("v")
                p.op("v", lambda e: e.tensor_tensor(
                    stat[:, 4:5], stat[:, 2:3], stat[:, 3:4], ALU.subtract),
                    inc=cv, k=1)
                p.wait("a", cv)
                p.op("a", lambda e: e.activation(
                    stat[:, 5:6], stat[:, 4:5], AF.Sqrt, bias=stat[:, 12:13], scale=1.0),
                    inc=ca, k=1)
                p.wait("v", ca)
                p.op("v", lambda e: e.reciprocal(stat[:, 6:7], stat[:, 5:6]))
                p.drain("v")
                p.waits("v", xn_gate)
                p.op("v", lambda e, b=xld2[bi]: e.tensor_scalar(
                    xnbf2[:, :], b[:, :], stat[:, 1:2], stat[:, 6:7],
                    ALU.add, ALU.mult), inc=cv, k=1)
                xld_gate[bi] = gate(cv, ca)
                p.wait("t", cv)
                p.waits("t", psTg[0])
                p.waits("t", psTg[1])
                for i in range(16):
                    pst = psT if (i % 8) < 4 else psT2
                    sl = (i % 4) * 128
                    p.tr(lambda e, i=i, pst=pst, sl=sl: e.transpose(
                        pst[0:128, sl:sl + 128], xnbf2[:, i * 128:(i + 1) * 128],
                        ident_sb[:, :]))
                    if i % 8 == 7:
                        p.inc_last("t", ct)
                        p.wait("a", ct)
                        for i2 in range(i - 7, i + 1):
                            ps_ = psT if (i2 % 8) < 4 else psT2
                            sl2 = (i2 % 4) * 128
                            p.op("a", lambda e, i2=i2, ps_=ps_, sl2=sl2, g=gt:
                                 e.copy(xnTo[:, i2, g * 128:(g + 1) * 128],
                                        ps_[0:128, sl2:sl2 + 128]))
                        p.op("a", lambda e: e.drain(), inc=ca, k=1)
                        if i == 7:
                            p.wait("t", ca)
                xn_gate = gate(ca)
                psTg[0] = gate(ca)
                psTg[1] = gate(ca)
                if gt == 3:
                    ffn_xn_half = gate(ca)

            ffn_xn_gate = gate(ca)
            gsl_gate = [[], []]
            dsl_gate = [[], []]
            psg2 = {i: [] for i in range(6)}
            sg_gate = []
            hT_gate = []
            yt_gate = []
            ssq_gate = []
            ybuf_war = []

            for tg in range(2):
                for j in range(44):
                    for half, ot in ((0, j), (1, 44 + j)):
                        bi = (2 * j + half) % 2
                        qn = "y"
                        p.waits(qn, gsl_gate[bi])
                        p.wait(qn, cc, 3)
                        p.op(qn, lambda e, b=gslab[bi], o=ot: e.dma_start(
                            out=b[:, :, :], in_=wdeq_gu.ap()[o, :, :]
                            .rearrange("q (s o) -> q s o", s=16)), inc=c_gs[bi], k=16)
                        pi = half
                        p.waits("t", psg2[pi])
                        p.wait("t", c_gs[bi])
                        p.waits("t", ffn_xn_gate)
                        for su in range(16):
                            p.mm(lambda e, b=gslab[bi], su=su, tg=tg, pi=pi:
                                 e.matmul(ps[pi][:, :], b[:, su, :],
                                          xnTo[:, su, tg * 512:(tg + 1) * 512],
                                          start=(su == 0), stop=(su == 15)))
                        p.inc_last("t", ct)
                        gsl_gate[bi] = gate(ct)
                        if half == 0:
                            p.wait("a", ct)
                            p.waits("a", sg_gate)
                            p.op("a", lambda e, pi=pi: e.activation(
                                sg_sb[:, :], ps[pi][:, :], AF.Silu),
                                inc=ca, k=1)
                            psg2[pi] = gate(ca)
                        else:
                            p.wait("v", ct)
                            p.wait("v", ca)
                            p.op("v", lambda e, j=j, pi=pi: e.tensor_tensor(
                                hT[:, j, :], sg_sb[:, :], ps[pi][:, :],
                                ALU.mult), inc=cv, k=1)
                            p.drain("v")
                            sg_gate = gate(cv)
                            psg2[pi] = gate(cv)
                hT_gate = gate(cv)
                for ot in range(16):
                    bi = ot % 2
                    qn = "y"
                    p.waits(qn, dsl_gate[bi])
                    p.wait(qn, cc, 4)
                    p.op(qn, lambda e, b=dslab[bi], o=ot: e.dma_start(
                        out=b[:, :, :], in_=wdeq_dn.ap()[o, :, :]
                        .rearrange("q (s o) -> q s o", s=44)), inc=c_ds[bi], k=16)
                    pi = 4
                    p.waits("t", psg2[pi])
                    p.wait("t", c_ds[bi])
                    p.waits("t", hT_gate)
                    for su in range(44):
                        p.mm(lambda e, b=dslab[bi], su=su, pi=pi: e.matmul(
                            ps[pi][:, :], b[:, su, :], hT[:, su, :],
                            start=(su == 0), stop=(su == 43)))
                    p.inc_last("t", ct)
                    dsl_gate[bi] = gate(ct)
                    p.wait("a", ct)
                    p.waits("a", yt_gate)
                    p.op("a", lambda e, pi=pi: e.copy(ytp[:, :], ps[pi][:, :]),
                         inc=ca, k=1)
                    psg2[pi] = gate(ca)
                    p.wait("t", ca)
                    p.waits("t", psg2[5])
                    for tt in range(4):
                        p.tr(lambda e, tt=tt: e.transpose(
                            ps[5][:, tt * 128:(tt + 1) * 128],
                            ytp[:, tt * 128:(tt + 1) * 128], ident32_sb[:, :]))
                    p.inc_last("t", ct)
                    p.wait("a", ct)
                    p.waits("a", ybuf_war)
                    for tt in range(4):
                        p.op("a", lambda e, tt=tt, o=ot: e.copy(
                            ybuf[:, tt, o * 128:(o + 1) * 128],
                            ps[5][:, tt * 128:(tt + 1) * 128]))
                    p.op("a", lambda e: e.drain(), inc=ca, k=1)
                    yt_gate = gate(ca)
                    psg2[5] = gate(ca)
                p.wait("a", ca)
                p.waits("a", ssq_gate)
                for tt in range(4):
                    p.op("a", lambda e, tt=tt: e.activation(
                        sq2[:, :], ybuf[:, tt, :], AF.Square,
                        accum_out=stat[:, 24 + tt:25 + tt]))
                    p.drain("a")
                p.op("a", lambda e: e.activation(
                    stat[:, 32:36], stat[:, 24:28], AF.Sqrt, bias=stat[:, 13:14],
                    scale=1.0 / HID), inc=ca, k=1)
                p.wait("v", ca)
                p.op("v", lambda e: e.reciprocal(stat[:, 36:40], stat[:, 32:36]))
                p.drain("v")
                p.op("v", lambda e: e.engine_nop(), inc=cv, k=1)
                ssq_gate = gate(cv)
                ynb_gate = []
                gbuf_gate = []
                st_gate = []
                for tt in range(4):
                    gt = tg * 4 + tt
                    p.waits("v", ynb_gate)
                    p.op("v", lambda e, tt=tt: e.tensor_scalar(
                        ynb[:, :], ybuf[:, tt, :], stat[:, 36 + tt:37 + tt],
                        None, ALU.mult), inc=cv, k=1)
                    p.wait("a", cv)
                    p.waits("a", gbuf_gate)
                    p.op("a", lambda e: e.activation(
                        gbuf[:, :], ynb[:, :], AF.Silu), inc=ca, k=1)
                    ynb_gate = gate(ca)
                    bi = tt % 2
                    p.waits("y", xld_gate[bi])
                    p.op("y", lambda e, b=xld2[bi], g=gt: e.dma_start(
                        out=b[:, :], in_=x_own[g * 128:(g + 1) * 128, :]),
                        inc=c_x2[bi], k=16)
                    p.wait("y", cc)
                    p.waits("y", st_gate)
                    p.op("y", lambda e, g=gt: e.dma_start(
                        out=st_sb[:, :], in_=s_own_d[g * 128:(g + 1) * 128, :]),
                        inc=c_st, k=16)
                    p.wait("v", c_x2[bi]); p.wait("v", c_st)
                    p.wait("v", ca)
                    p.op("v", lambda e: e.tensor_tensor(
                        st_sb[:, :], st_sb[:, :], bo_sb[:, :], ALU.add))
                    p.drain("v")
                    p.op("v", lambda e: e.tensor_tensor(
                        st_sb[:, :], st_sb[:, :], gbuf[:, :], ALU.mult))
                    p.drain("v")
                    p.op("v", lambda e, b=xld2[bi]: e.tensor_tensor(
                        ot_sb[:, :], b[:, :], st_sb[:, :], ALU.add),
                        inc=cv, k=1)
                    xld_gate[bi] = gate(cv)
                    gbuf_gate = gate(cv)
                    st_gate = gate(cv)
                    p.wait("a", cv)
                    p.op("a", lambda e, g=gt: e.dma_start(
                        out=out_own[g * 128:(g + 1) * 128, :], in_=ot_sb[:, :]),
                        inc=cg, k=16)
                    p.wait("v", cg, cg.n)
                ssq_gate = gate(cv)
                ybuf_war = gate(cv)

            if debug_outputs:
                p.op("a", lambda e: e.dma_start(
                    out=dbg["sown"][:, :], in_=s_own_d[:, :]), inc=cdbg, k=16)

            p.wait("a", cg)
            p.barrier(allc)
            with nc.named_scope("E"), nc.Block() as blk:
                p.flush(blk)

    return nc



def _host_inputs(inputs):
    x = np.ascontiguousarray(np.asarray(inputs["L0_attn_x"], dtype=np.float32))
    wq_down = np.asarray(inputs["L0_attn_Wq_down"], np.float32)
    wq_up = np.asarray(inputs["L0_attn_Wq_up"], np.float32)
    wq_rope = np.asarray(inputs["L0_attn_Wq_rope"], np.float32)
    wkv_down = np.asarray(inputs["L0_attn_Wkv_down"], np.float32)
    wk_up = np.asarray(inputs["L0_attn_Wk_up"], np.float32)
    wk_rope = np.asarray(inputs["L0_attn_Wk_rope"], np.float32)
    wv_up = np.asarray(inputs["L0_attn_Wv_up"], np.float32)
    wo = np.asarray(inputs["L0_attn_Wo"], np.float32)
    bo = np.asarray(inputs["L0_attn_bo"], np.float32)
    gup = np.ascontiguousarray(np.asarray(inputs["L0_ffn_gate_up_packed"], np.int32))
    gus = np.asarray(inputs["L0_ffn_gate_up_scales"], np.float16)
    dnp = np.ascontiguousarray(np.asarray(inputs["L0_ffn_down_packed"], np.int32))
    dns = np.asarray(inputs["L0_ffn_down_scales"], np.float16)

    c = lambda a: np.ascontiguousarray(a)

    inv_freq = 1.0 / 10000.0 ** (np.arange(0, R, 2, dtype=np.float64) / R)
    t = np.arange(S, dtype=np.float64)
    freqs = t[:, None] * inv_freq[None, :]
    emb = np.concatenate([freqs, freqs], axis=-1)
    cos = np.cos(emb).astype(np.float32)
    sin = np.sin(emb).astype(np.float32)
    sin_adj = sin.copy()
    sin_adj[:, :16] = -sin_adj[:, :16]
    cos_t = c(np.tile(cos, (1, 4)))
    sin_t = c(np.tile(sin_adj, (1, 4)))
    ki = np.arange(128)[:, None]
    cc_ = np.arange(896)[None, :]
    mask = np.where((cc_ - 384) >= ki, 0.0, -10000.0).astype(np.float32)
    ident = np.eye(128, dtype=NPBF16)
    ident32 = np.eye(128, dtype=np.float32)
    bo_bc = c(np.tile(bo[None, :], (128, 1)))

    gus_T = np.ascontiguousarray(gus.T)
    dns_T = np.ascontiguousarray(dns.T)
    shared = {
        "wqd_t": c(wq_down.T), "wkvd_t": c(wkv_down.T),
        "wkrope_t": c(wk_rope.T), "bo_bc": bo_bc,
        "cos_t": cos_t, "sin_t": sin_t, "mask_in": mask,
        "ident_in": ident, "ident32_in": ident32,
    }
    maps = []
    for core in range(8):
        b, hg = core // 4, core % 4
        hs = slice(4 * hg * CDIM, (4 * hg + 4) * CDIM)
        rs = slice(4 * hg * R, (4 * hg + 4) * R)
        ds = slice(4 * hg * DH, (4 * hg + 4) * DH)
        m = dict(shared)
        m["xb_in"] = c(x[b])
        m["x_own"] = c(x[b, 1024 * hg:1024 * (hg + 1)])
        m["cos_own"] = c(cos_t[1024 * hg:1024 * (hg + 1)])
        m["sin_own"] = c(sin_t[1024 * hg:1024 * (hg + 1)])
        m["wqup_t"] = c(wq_up[hs].T)
        m["wkup_t"] = c(wk_up[hs].T)
        m["wqrope_t"] = c(wq_rope[rs].T)
        m["wvup_t"] = c(wv_up[ds].T)
        m["wo_t_own"] = c(wo.T[ds])
        m["gup_in"] = c(gup[core * 1408:(core + 1) * 1408])
        m["gus_t"] = c(gus_T[core * 1408:(core + 1) * 1408])
        m["dnp_in"] = c(dnp[core * 256:(core + 1) * 256])
        m["dns_t"] = c(dns_T[core * 256:(core + 1) * 256])
        maps.append(m)
    return maps


_NC_CACHE = {}


def kernel(**inputs):
    maps = _host_inputs(inputs)
    if "nc" not in _NC_CACHE:
        _NC_CACHE["nc"] = build_nc(debug_outputs=False)
    nc = _NC_CACHE["nc"]
    res = run_bass_kernel_spmd(nc, maps, core_ids=list(range(8)))
    out = np.zeros((B, S, HID), dtype=np.float32)
    for core in range(8):
        b, hg = core // 4, core % 4
        out[b, 1024 * hg:1024 * (hg + 1)] = res.results[core]["out_own"]
    return out


# version 15
# speedup vs baseline: 1.0094x; 1.0094x over previous
import math
from contextlib import ExitStack

import numpy as np
import ml_dtypes

import concourse.bass as bass
import concourse.mybir as mybir
from concourse.bass_utils import run_bass_kernel_spmd

F32 = mybir.dt.float32
F16 = mybir.dt.float16
BF16 = mybir.dt.bfloat16
I32 = mybir.dt.int32
U16 = mybir.dt.uint16
AF = mybir.ActivationFunctionType
ALU = mybir.AluOpType
AX = mybir.AxisListType.X
NPBF16 = ml_dtypes.bfloat16

B, S, HID, NH, LAT, R = 2, 4096, 2048, 16, 512, 32
DH, CDIM = 128, 96
FI = 5632
ISD = 1.0 / math.sqrt(DH)
GROUPS = [[0, 1, 2, 3], [4, 5, 6, 7]]
TWO126 = float(2.0 ** 126)
ENGINES = ("t", "v", "a", "g", "y")


class Ctr:
    def __init__(self, nc, name):
        self.h = nc.alloc_semaphore(name=name)
        self.n = 0


def gate(*ctrs):
    return [(c.h, c.n) for c in ctrs]


class Prog:
    def __init__(self, nc):
        self.nc = nc
        self.steps = {e: [] for e in ENGINES}
        self.pe_was_mm = False

    def wait(self, eng, ctr, val=None):
        v = ctr.n if val is None else val
        if v > 0:
            self.steps[eng].append(("w", ctr.h, v))

    def waits(self, eng, g):
        for (h, v) in g:
            if v > 0:
                self.steps[eng].append(("w", h, v))

    def op(self, eng, fn, inc=None, k=1):
        if inc is not None:
            inc.n += k
            self.steps[eng].append(("o", fn, inc.h, k))
        else:
            self.steps[eng].append(("o", fn, None, 0))

    def drain(self, eng):
        self.steps[eng].append(("d",))

    def inc_last(self, eng, ctr, k=1):
        for i in range(len(self.steps[eng]) - 1, -1, -1):
            st = self.steps[eng][i]
            if st[0] == "o":
                assert st[2] is None, "op already carries an inc"
                ctr.n += k
                self.steps[eng][i] = ("o", st[1], ctr.h, k)
                return
        raise AssertionError("no op to attach inc to")

    def mm(self, fn, inc=None, k=1):
        self.op("t", fn, inc, k)
        self.pe_was_mm = True

    def tr(self, fn, inc=None, k=1):
        if self.pe_was_mm:
            self.steps["t"].append(("n",))
            self.pe_was_mm = False
        self.op("t", fn, inc, k)

    def barrier(self, ctrs):
        for e in ENGINES:
            for c in ctrs:
                self.wait(e, c)

    def flush(self, block):
        emap = {"t": block.tensor, "v": block.vector, "a": block.scalar,
                "g": block.gpsimd, "y": block.sync}
        for e in ENGINES:
            steps = self.steps[e]
            if not steps:
                continue

            def body(eng, steps=steps):
                for st in steps:
                    if st[0] == "w":
                        eng.wait_ge(st[1], st[2])
                    elif st[0] == "d":
                        eng.drain()
                    elif st[0] == "n":
                        eng.nop()
                    else:
                        inst = st[1](eng)
                        if st[2] is not None:
                            inst.then_inc(st[2], st[3])
            emap[e](body)
        self.steps = {e: [] for e in ENGINES}
        self.pe_was_mm = False


def build_nc(debug_outputs=False, ablate=frozenset()):
    nc = bass.Bass("TRN2", target_bir_lowering=False,
                   dynamic_dma_scratch_size=2048)
    p = Prog(nc)

    def din(name, shape, dt):
        return nc.dram_tensor(name, list(shape), dt, kind="ExternalInput")

    xb_in = din("xb_in", [S, HID], F32)
    x_own = din("x_own", [1024, HID], F32)
    wqd_t = din("wqd_t", [HID, LAT], F32)
    wkvd_t = din("wkvd_t", [HID, LAT], F32)
    wqup_t = din("wqup_t", [LAT, 4 * CDIM], F32)
    wkup_t = din("wkup_t", [LAT, 4 * CDIM], F32)
    wqrope_t = din("wqrope_t", [LAT, 4 * R], F32)
    wkrope_t = din("wkrope_t", [HID, R], F32)
    wvup_t = din("wvup_t", [LAT, 4 * DH], F32)
    wo_t_own = din("wo_t_own", [4 * DH, HID], F32)
    bo_bc = din("bo_bc", [128, HID], F32)
    cos_t = din("cos_t", [S, 128], F32)
    sin_t = din("sin_t", [S, 128], F32)
    cos_own = din("cos_own", [1024, 128], F32)
    sin_own = din("sin_own", [1024, 128], F32)
    mask_in = din("mask_in", [128, 896], F32)
    ident_in = din("ident_in", [128, 128], BF16)
    ident32_in = din("ident32_in", [128, 128], F32)
    gup_in = din("gup_in", [11 * 128, 256], I32)
    gus_t = din("gus_t", [11 * 128, 16], F16)
    dnp_in = din("dnp_in", [2 * 128, 704], I32)
    dns_t = din("dns_t", [2 * 128, 44], F16)

    out_own = nc.dram_tensor("out_own", [1024, HID], F32, kind="ExternalOutput")

    gu_contrib = nc.dram_tensor("gu_contrib", [11, 128, 2048], BF16)
    dn_contrib = nc.dram_tensor("dn_contrib", [2, 128, 5632], BF16)
    wdeq_gu = nc.dram_tensor("wdeq_gu", [88, 128, 2048], BF16)
    wdeq_dn = nc.dram_tensor("wdeq_dn", [16, 128, 5632], BF16)
    sp_in = nc.dram_tensor("sp_in", [S, HID], F32)
    lat_c = nc.dram_tensor("lat_c", [128, 8192], BF16)
    lat_g = nc.dram_tensor("lat_g", [512, 8192], BF16)
    kr_c = nc.dram_tensor("kr_c", [32, 1024], BF16)
    kr_g = nc.dram_tensor("kr_g", [128, 1024], BF16)
    s_own_d = nc.dram_tensor("s_own_d", [1024, HID], F32)

    dbg = {}
    if debug_outputs:
        dbg["qT"] = nc.dram_tensor("dbg_qT", [128, 4 * S], BF16, kind="ExternalOutput")
        dbg["kT"] = nc.dram_tensor("dbg_kT", [128, 4 * S], BF16, kind="ExternalOutput")
        dbg["v"] = nc.dram_tensor("dbg_v", [128, 32 * 512], BF16, kind="ExternalOutput")
        dbg["ctxT"] = nc.dram_tensor("dbg_ctxT", [128, 4 * S], BF16, kind="ExternalOutput")
        dbg["sown"] = nc.dram_tensor("dbg_sown", [1024, HID], F32, kind="ExternalOutput")
        dbg["wqd"] = nc.dram_tensor("dbg_wqd", [128, 16 * 512], BF16, kind="ExternalOutput")
        dbg["xnT"] = nc.dram_tensor("dbg_xnT", [128, 16 * 512], BF16, kind="ExternalOutput")
        dbg["qlat"] = nc.dram_tensor("dbg_qlat", [128, 4 * 512], BF16, kind="ExternalOutput")
        dbg["xnT0"] = nc.dram_tensor("dbg_xnT0", [128, 16 * 512], BF16, kind="ExternalOutput")
        dbg["xld7"] = nc.dram_tensor("dbg_xld7", [128, HID], F32, kind="ExternalOutput")

    cy = Ctr(nc, "cy")
    c_w = Ctr(nc, "c_w")
    c_pk = [Ctr(nc, "c_pk0"), Ctr(nc, "c_pk1")]
    c_sc = Ctr(nc, "c_sc")
    c_x = Ctr(nc, "c_x")
    c_x2 = [Ctr(nc, "c_x2a"), Ctr(nc, "c_x2b")]
    c_cs = [Ctr(nc, "c_cs0"), Ctr(nc, "c_cs1")]
    c_gs = [Ctr(nc, "c_gs0"), Ctr(nc, "c_gs1")]
    c_ds = [Ctr(nc, "c_ds0"), Ctr(nc, "c_ds1")]
    c_st = Ctr(nc, "c_st")
    c_sp = [Ctr(nc, "c_sp0"), Ctr(nc, "c_sp1")]
    c_lat = Ctr(nc, "c_lat")
    c_kr = Ctr(nc, "c_kr")
    c_lq = [Ctr(nc, "c_lq0"), Ctr(nc, "c_lq1")]
    cg = Ctr(nc, "cg")
    cdbg = Ctr(nc, "cdbg")
    cc = Ctr(nc, "cc")
    ct = Ctr(nc, "ct")
    cv = Ctr(nc, "cv")
    ca = Ctr(nc, "ca")
    allc = [cy, c_w, c_pk[0], c_pk[1], c_sc, c_x, c_x2[0], c_x2[1], c_cs[0], c_cs[1],
            c_gs[0], c_gs[1], c_ds[0], c_ds[1], c_st, c_sp[0], c_sp[1],
            c_lat, c_kr, c_lq[0], c_lq[1], cg, cdbg, ct, cv, ca]

    def sb(ctx, name, shape, dt):
        return ctx.enter_context(nc.sbuf_tensor(name, list(shape), dt))

    with ExitStack() as ctx0:
        ps = [ctx0.enter_context(nc.psum_tensor(f"ps{i}", [128, 512], F32))
              for i in range(6)]
        psT = ctx0.enter_context(nc.psum_tensor("psT", [128, 512], BF16))
        psT2 = ctx0.enter_context(nc.psum_tensor("psT2", [128, 512], BF16))

        ident_sb = sb(ctx0, "ident_sb", [128, 128], BF16)
        bo_sb = sb(ctx0, "bo_sb", [128, HID], F32)
        stat = sb(ctx0, "stat", [128, 64], F32)

        wspec = [("wqd", wqd_t, 16, LAT), ("wkvd", wkvd_t, 16, LAT),
                 ("wqup", wqup_t, 4, 4 * CDIM), ("wkup", wkup_t, 4, 4 * CDIM),
                 ("wqrope", wqrope_t, 4, 4 * R), ("wkrope", wkrope_t, 16, R),
                 ("wvup", wvup_t, 4, 4 * DH)]

        with ExitStack() as ctxAB:
            qT = sb(ctxAB, "qT", [128, 4, S], BF16)
            kT = sb(ctxAB, "kT", [128, 4, S], BF16)
            v_sb = sb(ctxAB, "v_sb", [128, 32, 512], BF16)
            wo_sb = sb(ctxAB, "wo_sb", [128, 4, HID], BF16)

            scrD = sb(ctxAB, "scrD", [128, 4608], F32)
            dq_bits = scrD[:, 0:1024].bitcast(U16)
            dq_sh = scrD[:, 1024:1280].bitcast(U16)
            dq_hi = scrD[:, 1280:1536].bitcast(U16)
            dq_lo = scrD[:, 1536:1792].bitcast(U16)
            dq_pk = [scrD[:, 1792:2048].bitcast(I32),
                     scrD[:, 2048:2304].bitcast(I32)]
            dq_sc = scrD[:, 2304:2312].bitcast(F16)
            dq_s32 = scrD[:, 2320:2336]
            dq_wscl = scrD[:, 2336:3360].bitcast(BF16).rearrange(
                "q (g o) -> q g o", g=16)
            dq_wob = scrD[:, 3360:4384].bitcast(BF16).rearrange(
                "q (g o) -> q g o", g=16)

            ctxWA = ExitStack()
            wsb = {}
            for name, src, nsub, m in wspec:
                wsb[name] = sb(ctxWA, name + "_sb", [128, nsub, m], BF16)
            wsb["wo"] = wo_sb

            with ExitStack() as ctxP0:
                scr0 = sb(ctxP0, "scr0", [128, 8192], F32)
                wf32 = scr0[:, 0:8192]

                for dst, src in ((ident_sb, ident_in),
                                 (bo_sb, bo_bc)):
                    p.op("y", lambda e, d=dst, s_=src:
                         e.dma_start(out=d[:, :], in_=s_[:, :]), inc=c_w, k=16)

                p.op("v", lambda e: e.memset(stat[:, 12:13], 1e-5))
                p.op("v", lambda e: e.memset(stat[:, 13:14], 1e-6))
                p.drain("v")

                cg_gate = []
                for name, src, nsub, m in (wspec + [("wo", wo_t_own, 4, HID)]):
                    dst = wsb[name]
                    ap_in = src.ap().rearrange("(s q) m -> q s m", q=128)
                    p.waits("y", cg_gate)
                    p.op("y", lambda e, a=ap_in, n=nsub, mm_=m:
                         e.dma_start(out=wf32[:, 0:n * mm_].rearrange(
                             "q (s m) -> q s m", s=n), in_=a), inc=c_w, k=16)
                    p.wait("v", c_w)
                    p.op("v", lambda e, d=dst, n=nsub, mm_=m:
                         e.tensor_copy(d[:, :, :], wf32[:, 0:n * mm_].rearrange(
                             "q (s m) -> q s m", s=n)), inc=cv, k=1)
                    p.drain("v")
                    cg_gate = gate(cv)

                p.barrier(allc)
                with nc.named_scope("P0"), nc.Block() as blk:
                    p.flush(blk)

            with ExitStack() as ctxA:
                cosb = sb(ctxA, "cosb", [128, 2, 128], F32)
                sinb = sb(ctxA, "sinb", [128, 2, 128], F32)
                rsc = sb(ctxA, "rsc", [128, 128], F32)
                rsw = sb(ctxA, "rsw", [128, 128], F32)
                ropebf = sb(ctxA, "ropebf", [128, 128], BF16)

                rsc_gate = []
                ropebf_gate = []
                cs_gate = [[], []]
                psg = {i: [] for i in range(6)}
                psTg = []
                psT2g = []

                with ExitStack() as ctxOwn:
                    xld = sb(ctxOwn, "xld0", [128, HID], F32)
                    xnbf = sb(ctxOwn, "xnbf", [128, HID], BF16)
                    xnh = sb(ctxOwn, "xnh", [128, 16, 512], BF16)
                    latst = sb(ctxOwn, "latst", [128, 4, 512], BF16)
                    krT_sb = sb(ctxOwn, "krT_sb", [32, 8, 128], BF16)

                    xld_gate = []
                    xn_gate = []
                    xnh_war = []
                    latst_gate = []
                    for half in range(2):
                        for tt in range(4):
                            gt = half * 4 + tt
                            p.waits("y", xld_gate)
                            p.op("y", lambda e, g=gt:
                                 e.dma_start(out=xld[:, :],
                                             in_=x_own[g * 128:(g + 1) * 128, :]),
                                 inc=c_x, k=16)
                            p.wait("v", c_x)
                            for sg4 in range(4):
                                p.op("v", lambda e, sg4=sg4: e.bn_stats(
                                    stat[:, 16 + 6 * sg4:22 + 6 * sg4],
                                    xld[:, 512 * sg4:512 * (sg4 + 1)]))
                                p.drain("v")
                            p.op("v", lambda e: e.bn_aggr(
                                stat[:, 0:2], stat[:, 16:40].rearrange(
                                    "q (g s) -> q g s", g=4)))
                            p.drain("v")
                            p.op("v", lambda e: e.tensor_scalar(
                                stat[:, 2:3], stat[:, 0:1], -1.0, None,
                                ALU.mult), inc=cv, k=1)
                            p.wait("a", cv)
                            p.op("a", lambda e: e.activation(
                                stat[:, 5:6], stat[:, 1:2], AF.Sqrt,
                                bias=stat[:, 12:13], scale=1.0), inc=ca, k=1)
                            p.wait("v", ca)
                            p.op("v", lambda e: e.reciprocal(
                                stat[:, 6:7], stat[:, 5:6]))
                            p.drain("v")
                            p.waits("v", xn_gate)
                            p.op("v", lambda e: e.tensor_scalar(
                                xnbf[:, :], xld[:, :], stat[:, 2:3],
                                stat[:, 6:7], ALU.add, ALU.mult), inc=cv, k=1)
                            xld_gate = gate(cv)
                            p.wait("t", cv)
                            for b4 in range(4):
                                pst = psT if b4 % 2 == 0 else psT2
                                if b4 % 2 == 0:
                                    p.waits("t", psTg)
                                else:
                                    p.waits("t", psT2g)
                                for i in range(b4 * 4, b4 * 4 + 4):
                                    sl = (i % 4) * 128
                                    p.tr(lambda e, i=i, sl=sl, pst=pst:
                                         e.transpose(
                                             pst[0:128, sl:sl + 128],
                                             xnbf[:, i * 128:(i + 1) * 128],
                                             ident_sb[:, :]))
                                p.inc_last("t", ct)
                                p.wait("a", ct)
                                if tt == 0 and b4 == 0:
                                    p.waits("a", xnh_war)
                                for i in range(b4 * 4, b4 * 4 + 4):
                                    sl = (i % 4) * 128
                                    p.op("a", lambda e, i=i, sl=sl, t=tt,
                                         pst=pst:
                                         e.copy(xnh[:, i, t * 128:(t + 1) * 128],
                                                pst[0:128, sl:sl + 128]))
                                p.op("a", lambda e: e.drain(), inc=ca, k=1)
                                if b4 % 2 == 0:
                                    psTg = gate(ca)
                                else:
                                    psT2g = gate(ca)
                            xn_gate = gate(ca)
                        xnh_gate = gate(ca)

                        p.waits("t", xnh_gate)
                        for wi, wname in ((0, "wqd"), (1, "wkvd")):
                            for lt in range(4):
                                pi = lt % 2
                                p.waits("t", psg[pi])
                                for su in range(16):
                                    p.mm(lambda e, w=wsb[wname], su=su, lt=lt,
                                         pi=pi: e.matmul(
                                             ps[pi][:, :],
                                             w[:, su, lt * 128:(lt + 1) * 128],
                                             xnh[:, su, :], start=(su == 0),
                                             stop=(su == 15)))
                                p.inc_last("t", ct)
                                p.wait("a", ct)
                                if lt == 0:
                                    p.waits("a", latst_gate)
                                p.op("a", lambda e, lt=lt, pi=pi: e.copy(
                                    latst[:, lt, :], ps[pi][:, :]), inc=ca, k=1)
                                psg[pi] = gate(ca)
                            p.wait("a", ca)
                            p.op("a", lambda e, wi=wi, half=half: e.dma_start(
                                out=lat_c.ap()[:, wi * 4096:(wi + 1) * 4096]
                                .rearrange("p (lt t) -> p lt t", lt=4)
                                [:, :, half * 512:(half + 1) * 512],
                                in_=latst[:, :, :]), inc=c_lat, k=16)
                            latst_gate = gate(c_lat)

                        for tt in range(4):
                            t8 = half * 4 + tt
                            cslot = t8 % 2
                            p.waits("y", cs_gate[cslot])
                            p.op("y", lambda e, g=t8, t=cslot:
                                 e.dma_start(out=cosb[:, t, :],
                                             in_=cos_own[g * 128:(g + 1) * 128, :]),
                                 inc=c_cs[cslot], k=16)
                            p.op("y", lambda e, g=t8, t=cslot:
                                 e.dma_start(out=sinb[:, t, :],
                                             in_=sin_own[g * 128:(g + 1) * 128, :]),
                                 inc=c_cs[cslot], k=16)
                            pi = 5
                            p.waits("t", psg[pi])
                            for su in range(16):
                                p.mm(lambda e, su=su, t=tt, pi=pi:
                                     e.matmul(
                                         ps[pi][:, 0:R],
                                         xnh[:, su, t * 128:(t + 1) * 128],
                                         wsb["wkrope"][:, su, :],
                                         start=(su == 0), stop=(su == 15)))
                            p.inc_last("t", ct)
                            p.wait("a", ct)
                            p.waits("a", rsc_gate)
                            p.op("a", lambda e, pi=pi: e.copy(
                                rsc[:, 0:R], ps[pi][:, 0:R]), inc=ca, k=1)
                            psg[pi] = gate(ca)
                            p.wait("v", ca)
                            p.wait("v", c_cs[cslot])
                            p.op("v", lambda e: e.tensor_copy(
                                rsw[:, 0:16], rsc[:, 16:32]))
                            p.op("v", lambda e: e.tensor_copy(
                                rsw[:, 16:32], rsc[:, 0:16]))
                            p.drain("v")
                            p.op("v", lambda e, t=cslot: e.tensor_tensor(
                                rsc[:, 0:R], rsc[:, 0:R], cosb[:, t, 0:R],
                                ALU.mult))
                            p.op("v", lambda e, t=cslot: e.tensor_tensor(
                                rsw[:, 0:R], rsw[:, 0:R], sinb[:, t, 0:R],
                                ALU.mult))
                            p.drain("v")
                            p.waits("v", ropebf_gate)
                            p.op("v", lambda e: e.tensor_tensor(
                                ropebf[:, 0:R], rsc[:, 0:R], rsw[:, 0:R],
                                ALU.add), inc=cv, k=1)
                            rsc_gate = gate(cv)
                            cs_gate[cslot] = gate(cv)
                            p.wait("t", cv)
                            p.waits("t", psT2g)
                            p.tr(lambda e: e.transpose(
                                psT2[0:R, 0:128], ropebf[:, 0:R],
                                ident_sb[:, :]), inc=ct, k=1)
                            ropebf_gate = gate(ct)
                            p.wait("a", ct)
                            p.op("a", lambda e, g=t8: e.copy(
                                krT_sb[:, g, :], psT2[0:R, 0:128]))
                            p.op("a", lambda e: e.drain(), inc=ca, k=1)
                            psT2g = gate(ca)
                        xnh_war = gate(ct)
                    p.wait("a", ca)
                    p.op("a", lambda e: e.dma_start(
                        out=kr_c.ap()[:, :],
                        in_=krT_sb[:, :, :].rearrange("p a b -> p (a b)")),
                        inc=c_lat, k=16)

                    p.wait("g", c_lat)
                    p.op("g", lambda e: e.collective_compute(
                        "AllGather", ALU.bypass, replica_groups=GROUPS,
                        ins=[lat_c.ap().opt()], outs=[lat_g.ap().opt()]),
                        inc=cc, k=1)
                    p.op("g", lambda e: e.collective_compute(
                        "AllGather", ALU.bypass, replica_groups=GROUPS,
                        ins=[kr_c.ap().opt()], outs=[kr_g.ap().opt()]),
                        inc=cc, k=1)

                p.wait("y", cc, 2)
                for c4 in range(4):
                    for h in range(4):
                        p.op("y", lambda e, c4=c4, h=h: e.dma_start(
                            out=kT[CDIM:128, h, c4 * 1024:(c4 + 1) * 1024],
                            in_=kr_g.ap()[c4 * 32:(c4 + 1) * 32, :]),
                            inc=c_kr, k=16)


                def dequant_gen():
                    pk_gate = [[], []]
                    sc_gate = []
                    dec_gate = []
                    wob_gate = []
                    units = [("gu", ot, 0) for ot in range(11)] +                             [("dn", ot, qq) for ot in range(2)
                             for qq in range(4)]
                    for ui, (kind, ot, qq) in enumerate(units):
                        Wu = 256 if kind == "gu" else 176
                        U = 2 * Wu
                        Gd = Wu // 16
                        srcp = gup_in if kind == "gu" else dnp_in
                        ssrc = gus_t if kind == "gu" else dns_t
                        ddst = gu_contrib if kind == "gu" else dn_contrib
                        buf = dq_pk[ui % 2]
                        p.waits("y", pk_gate[ui % 2])
                        p.op("y", lambda e, b=buf, s_=srcp, o=ot, q=qq, w=Wu:
                             e.dma_start(out=b[:, 0:w],
                                         in_=s_[o * 128:(o + 1) * 128,
                                                q * w:(q + 1) * w]),
                             inc=c_pk[ui % 2], k=16)
                        p.waits("y", sc_gate)
                        p.op("y", lambda e, s_=ssrc, o=ot, q=qq, g=Gd:
                             e.dma_start(out=dq_sc[:, 0:g],
                                         in_=s_[o * 128:(o + 1) * 128,
                                                q * g:(q + 1) * g]),
                             inc=c_sc, k=16)
                        yield
                        p.wait("v", c_pk[ui % 2])
                        p.wait("v", c_sc)
                        p.waits("v", dec_gate)
                        p.op("v", lambda e, g=Gd: e.tensor_scalar(
                            dq_s32[:, 0:g], dq_sc[:, 0:g], TWO126, None,
                            ALU.mult))
                        u16v = buf[:, 0:Wu].bitcast(U16)
                        for j in range(4):
                            sv = u16v
                            if j > 0:
                                p.drain("v")
                                p.op("v", lambda e, u=u16v, jj=j, UU=U:
                                     e.tensor_scalar(
                                         dq_sh[:, 0:UU], u, 4 * jj, None,
                                         ALU.logical_shift_right))
                                p.drain("v")
                                sv = dq_sh[:, 0:U]
                            p.op("v", lambda e, s_=sv, UU=U: e.tensor_scalar(
                                dq_hi[:, 0:UU], s_, 8, 12, ALU.bitwise_and,
                                ALU.logical_shift_left))
                            p.op("v", lambda e, s_=sv, UU=U: e.tensor_scalar(
                                dq_lo[:, 0:UU], s_, 7, 6, ALU.bitwise_and,
                                ALU.logical_shift_left))
                            p.drain("v")
                            p.op("v", lambda e, jj=j, UU=U: e.tensor_tensor(
                                dq_bits[:, 0:4 * UU].rearrange(
                                    "q (u j) -> q u j", j=4)[:, :, jj],
                                dq_hi[:, 0:UU], dq_lo[:, 0:UU],
                                ALU.bitwise_or))
                            yield
                        p.drain("v")
                        bview = dq_bits[:, 0:4 * U].bitcast(BF16)
                        for g in range(Gd):
                            p.op("v", lambda e, g=g, bv=bview: e.tensor_scalar(
                                dq_wscl[:, g, :], bv[:, 128 * g:128 * (g + 1)],
                                dq_s32[:, g:g + 1], None, ALU.mult))
                            if g % 6 == 5:
                                yield
                        p.drain("v")
                        p.op("v", lambda e: e.engine_nop(), inc=cv, k=1)
                        wsc_cv = cv.n
                        pk_gate[ui % 2] = gate(cv)
                        sc_gate = gate(cv)
                        yield
                        p.wait("t", cv, wsc_cv)
                        for g0 in range(0, Gd, 4):
                            gges = list(range(g0, min(g0 + 4, Gd)))
                            p.wait("t", ca)
                            for g in gges:
                                sl = (g % 4) * 128
                                p.tr(lambda e, g=g, sl=sl: e.transpose(
                                    psT[0:128, sl:sl + 128], dq_wscl[:, g, :],
                                    ident_sb[:, :]))
                            p.inc_last("t", ct)
                            p.wait("a", ct)
                            if g0 == 0:
                                p.waits("a", wob_gate)
                            for g in gges:
                                sl = (g % 4) * 128
                                p.op("a", lambda e, g=g, sl=sl: e.copy(
                                    dq_wob[:, g, :], psT[0:128, sl:sl + 128]))
                            p.op("a", lambda e: e.drain(), inc=ca, k=1)
                            yield
                        dec_gate = gate(ct)
                        p.wait("a", ca)
                        p.op("a", lambda e, d=ddst, o=ot, q=qq, g=Gd:
                             e.dma_start(
                                 out=d.ap()[o, :,
                                            q * g * 128:(q + 1) * g * 128],
                                 in_=dq_wob[:, 0:g, :].rearrange(
                                     "q g o -> q (g o)")),
                             inc=cg, k=16)
                        wob_gate = gate(cg)
                        yield

                dq_gen = dequant_gen()
                dq_live = [True]

                def dq_step():
                    if dq_live[0]:
                        try:
                            next(dq_gen)
                        except StopIteration:
                            dq_live[0] = False

                for _ in range(3):
                    dq_step()

                qlatb = [sb(ctxA, f"qlatb{i}", [128, 4, 512], BF16)
                         for i in range(2)]
                kvlatb = [sb(ctxA, f"kvlatb{i}", [128, 4, 512], BF16)
                          for i in range(2)]
                lq_gate = [[], []]
                for ck in range(8):
                    bi = ck % 2
                    c4 = ck // 2
                    toff = (ck % 2) * 512
                    p.waits("y", lq_gate[bi])
                    p.wait("y", cc, 1)
                    p.op("y", lambda e, b=qlatb[bi], c4=c4, to=toff:
                         e.dma_start(out=b[:, :, :],
                                     in_=lat_g.ap()[c4 * 128:(c4 + 1) * 128,
                                                    0:4096]
                                     .rearrange("p (lt t) -> p lt t", lt=4)
                                     [:, :, to:to + 512]), inc=c_lq[bi], k=16)
                    p.op("y", lambda e, b=kvlatb[bi], c4=c4, to=toff:
                         e.dma_start(out=b[:, :, :],
                                     in_=lat_g.ap()[c4 * 128:(c4 + 1) * 128,
                                                    4096:8192]
                                     .rearrange("p (lt t) -> p lt t", lt=4)
                                     [:, :, to:to + 512]), inc=c_lq[bi], k=16)

                    p.wait("t", c_lq[bi])
                    for (wname, latb, dstT) in (("wkup", kvlatb[bi], kT),
                                                ("wqup", qlatb[bi], qT)):
                        for h in range(4):
                            pi = 2 + (h % 2)
                            p.waits("t", psg[pi])
                            for ls in range(4):
                                p.mm(lambda e, w=wsb[wname], ls=ls, h=h,
                                     la=latb, pi=pi: e.matmul(
                                         ps[pi][0:CDIM, :],
                                         w[:, ls, h * CDIM:(h + 1) * CDIM],
                                         la[:, ls, :], start=(ls == 0),
                                         stop=(ls == 3)))
                            p.inc_last("t", ct)
                            p.wait("a", ct)
                            p.op("a", lambda e, d=dstT, h=h, c=ck, pi=pi:
                                 e.copy(d[0:CDIM, h, c * 512:(c + 1) * 512],
                                        ps[pi][0:CDIM, :]), inc=ca, k=1)
                            psg[pi] = gate(ca)
                            dq_step()

                    for tt in range(4):
                        gt = ck * 4 + tt
                        pi = 4
                        p.waits("t", psg[pi])
                        for ls in range(4):
                            p.mm(lambda e, ls=ls, t=tt, pi=pi, la=kvlatb[bi]:
                                 e.matmul(
                                     ps[pi][:, :],
                                     la[:, ls, t * 128:(t + 1) * 128],
                                     wsb["wvup"][:, ls, :], start=(ls == 0),
                                     stop=(ls == 3)))
                        p.inc_last("t", ct)
                        p.wait("v", ct)
                        p.op("v", lambda e, g=gt, pi=pi: e.tensor_copy(
                            v_sb[:, g, :], ps[pi][:, :]), inc=cv, k=1)
                        p.drain("v")
                        psg[pi] = gate(cv)
                        dq_step()

                    for tt in range(4):
                        gt = ck * 4 + tt
                        cslot = gt % 2
                        p.waits("y", cs_gate[cslot])
                        p.op("y", lambda e, g=gt, t=cslot:
                             e.dma_start(out=cosb[:, t, :],
                                         in_=cos_t[g * 128:(g + 1) * 128, :]),
                             inc=c_cs[cslot], k=16)
                        p.op("y", lambda e, g=gt, t=cslot:
                             e.dma_start(out=sinb[:, t, :],
                                         in_=sin_t[g * 128:(g + 1) * 128, :]),
                             inc=c_cs[cslot], k=16)
                        pi = 5
                        p.waits("t", psg[pi])
                        for ls in range(4):
                            p.mm(lambda e, ls=ls, t=tt, pi=pi, la=qlatb[bi]:
                                 e.matmul(
                                     ps[pi][:, 0:128],
                                     la[:, ls, t * 128:(t + 1) * 128],
                                     wsb["wqrope"][:, ls, :], start=(ls == 0),
                                     stop=(ls == 3)))
                        p.inc_last("t", ct)
                        p.wait("a", ct)
                        p.waits("a", rsc_gate)
                        p.op("a", lambda e, pi=pi: e.copy(
                            rsc[:, :], ps[pi][:, 0:128]), inc=ca, k=1)
                        psg[pi] = gate(ca)
                        p.wait("v", ca)
                        p.wait("v", c_cs[cslot])
                        p.op("v", lambda e: e.tensor_copy(
                            rsw[:, :].rearrange("q (h r) -> q h r", h=4)[:, :, 0:16],
                            rsc[:, :].rearrange("q (h r) -> q h r", h=4)[:, :, 16:32]))
                        p.op("v", lambda e: e.tensor_copy(
                            rsw[:, :].rearrange("q (h r) -> q h r", h=4)[:, :, 16:32],
                            rsc[:, :].rearrange("q (h r) -> q h r", h=4)[:, :, 0:16]))
                        p.drain("v")
                        p.op("v", lambda e, t=cslot: e.tensor_tensor(
                            rsc[:, :], rsc[:, :], cosb[:, t, :], ALU.mult))
                        p.op("v", lambda e, t=cslot: e.tensor_tensor(
                            rsw[:, :], rsw[:, :], sinb[:, t, :], ALU.mult))
                        p.drain("v")
                        p.waits("v", ropebf_gate)
                        p.op("v", lambda e: e.tensor_tensor(
                            ropebf[:, :], rsc[:, :], rsw[:, :], ALU.add),
                            inc=cv, k=1)
                        rsc_gate = gate(cv)
                        cs_gate[cslot] = gate(cv)
                        p.wait("t", cv)
                        p.waits("t", psT2g)
                        for h in range(4):
                            p.tr(lambda e, h=h: e.transpose(
                                psT2[0:R, h * 128:(h + 1) * 128],
                                ropebf[:, h * R:(h + 1) * R], ident_sb[:, :]))
                        p.inc_last("t", ct)
                        ropebf_gate = gate(ct)
                        p.wait("a", ct)
                        for h in range(4):
                            p.op("a", lambda e, h=h, g=gt: e.copy(
                                qT[CDIM:128, h, g * 128:(g + 1) * 128],
                                psT2[0:R, h * 128:(h + 1) * 128]))
                        p.op("a", lambda e: e.drain(), inc=ca, k=1)
                        psT2g = gate(ca)
                        if tt % 2 == 1:
                            dq_step()
                    lq_gate[bi] = gate(ct)
                p.wait("t", c_kr)

                if debug_outputs:
                    p.barrier(allc)
                    p.op("a", lambda e: e.dma_start(
                        out=dbg["wqd"][:, :],
                        in_=wsb["wqd"][:, :, :].rearrange("q h s -> q (h s)")),
                        inc=cdbg, k=16)
                    p.op("a", lambda e: e.dma_start(
                        out=dbg["qT"][:, :],
                        in_=qT[:, :, :].rearrange("q h s -> q (h s)")),
                        inc=cdbg, k=16)
                    p.op("a", lambda e: e.dma_start(
                        out=dbg["kT"][:, :],
                        in_=kT[:, :, :].rearrange("q h s -> q (h s)")),
                        inc=cdbg, k=16)
                    p.op("a", lambda e: e.dma_start(
                        out=dbg["v"][:, :],
                        in_=v_sb[:, :, :].rearrange("q h s -> q (h s)")),
                        inc=cdbg, k=16)

                p.barrier(allc)
                with nc.named_scope("A"), nc.Block() as blk:
                    p.flush(blk)

            ctxWA.close()

            with ExitStack() as ctxB:
                mask_sb = sb(ctxB, "mask_sb", [128, 896], F32)
                ctxT = sb(ctxB, "ctxT", [128, 4, S], BF16)
                p.op("y", lambda e: e.dma_start(out=mask_sb[:, :],
                                                in_=mask_in[:, :]),
                     inc=cy, k=16)
                p.wait("v", cy)
                pt_sb = sb(ctxB, "pt_sb", [128, 3, 512], BF16)
                sqb = sb(ctxB, "sqb", [128, 2, 512], BF16)
                nrm = sb(ctxB, "nrm", [1, 64], F32)
                onec = sb(ctxB, "onec", [128, 1], BF16)
                oner = sb(ctxB, "oner", [1, 128], F32)
                biasc = sb(ctxB, "biasc", [128, 4], F32)
                linvr = sb(ctxB, "linvr", [1, 512], F32)
                linvb = sb(ctxB, "linvb", [128, 512], F32)
                p.op("v", lambda e: e.memset(onec[:, :], 1.0))
                p.op("v", lambda e: e.memset(oner[:, :], 1.0))
                p.drain("v")
                p.op("v", lambda e: e.engine_nop(), inc=cv, k=1)
                ones_gate = gate(cv)


                G = {"psS": [[], []],
                     "pt": [[], [], []],
                     "sq": [[], []],
                     "ps5": [],
                     "l": [],
                     "ctx": [[], []],
                     "nrm": [],
                     "nrm2": []}
                bias_done = [0, 0, 0, 0]
                psc_g = {0: [], 1: [], 2: [], 3: []}
                pending_epi = [None]
                qg_idx = [0]

                def emit_epi(h, qg, cb, par, end_ct):
                    p.wait("v", ct, end_ct)
                    p.op("v", lambda e: e.reciprocal(
                        linvr[0:1, :], ps[3][0:1, :]), inc=cv, k=1)
                    G["l"] = gate(cv)
                    p.wait("t", cv)
                    p.waits("t", G["ps5"])
                    p.mm(lambda e: e.matmul(
                        ps[5][:, :], oner[:, :], linvr[0:1, :],
                        start=True, stop=True), inc=ct, k=1)
                    p.wait("a", ct)
                    p.op("a", lambda e: e.copy(linvb[:, :], ps[5][:, :]),
                         inc=ca, k=1)
                    G["ps5"] = gate(ca)
                    p.wait("v", ca)
                    p.op("v", lambda e, h=h, qg=qg, cb=cb: e.tensor_tensor(
                        ctxT[:, h, qg * 512:(qg + 1) * 512],
                        cb[:, :], linvb[:, :], ALU.mult), inc=cv, k=1)
                    G["ctx"][par] = gate(cv)

                for h in range(4):
                    if pending_epi[0] is not None:
                        pending_epi[0]()
                        pending_epi[0] = None
                    rd_done = {}
                    for idx in range(16):
                        src_i, t8 = idx // 8, idx % 8
                        srcT = qT if src_i == 0 else kT
                        slot = idx % 2
                        row = 64 if (idx % 2) else 0
                        p.waits("v", G["sq"][slot])
                        p.waits("v", G["nrm"])
                        p.op("v", lambda e, h=h, s=srcT, t8=t8, slot=slot:
                             e.tensor_tensor(
                                 sqb[:, slot, :],
                                 s[:, h, t8 * 512:(t8 + 1) * 512],
                                 s[:, h, t8 * 512:(t8 + 1) * 512], ALU.mult),
                             inc=cv, k=1)
                        p.wait("t", cv)
                        if idx == 0:
                            p.waits("t", G["ps5"])
                            if h == 0:
                                p.waits("t", ones_gate)
                        if idx >= 2:
                            p.wait("t", cv, rd_done[idx - 2])
                        p.mm(lambda e, slot=slot, row=row: e.matmul(
                            ps[5][row:row + 1, :], onec[:, :],
                            sqb[:, slot, :],
                            start=True, stop=True), inc=ct, k=1)
                        G["sq"][slot] = gate(ct)
                        p.wait("v", ct)
                        p.op("v", lambda e, idx=idx, row=row: e.reduce_max(
                            nrm[0:1, idx:idx + 1],
                            ps[5][row:row + 1, :], AX), inc=cv, k=1)
                        rd_done[idx] = cv.n
                    G["ps5"] = gate(cv)
                    p.waits("v", G["nrm2"])
                    p.op("v", lambda e: e.reduce_max(
                        nrm[0:1, 16:17], nrm[0:1, 0:8], AX))
                    p.drain("v")
                    p.op("v", lambda e: e.reduce_max(
                        nrm[0:1, 17:18], nrm[0:1, 8:16], AX), inc=cv, k=1)
                    p.wait("a", cv)
                    p.op("a", lambda e: e.activation(
                        nrm[0:1, 20:22], nrm[0:1, 16:18], AF.Sqrt),
                        inc=ca, k=1)
                    G["nrm"] = gate(ca)
                    p.wait("v", ca)
                    p.op("v", lambda e: e.tensor_tensor(
                        nrm[0:1, 24:25], nrm[0:1, 20:21], nrm[0:1, 21:22],
                        ALU.mult))
                    p.drain("v")
                    p.op("v", lambda e: e.tensor_scalar(
                        nrm[0:1, 25:26], nrm[0:1, 24:25], -ISD, -2.0,
                        ALU.mult, ALU.add), inc=cv, k=1)
                    p.wait("t", cv)
                    p.waits("t", G["ps5"])
                    p.mm(lambda e: e.matmul(ps[5][0:128, 0:1], oner[:, :],
                                            nrm[0:1, 25:26],
                                            start=True, stop=True), inc=ct, k=1)
                    G["nrm2"] = gate(ct)
                    p.wait("a", ct)
                    p.op("a", lambda e, h=h: e.copy(
                        biasc[:, h:h + 1], ps[5][:, 0:1]), inc=ca, k=1)
                    bias_done[h] = ca.n
                    G["ps5"] = gate(ca)

                    for qg in range(8):
                        nkt = 4 * qg + 4
                        par = qg_idx[0] % 2
                        cb = ps[4] if par == 0 else ps[2]
                        s_done = {}
                        m_done = {}
                        exp_done = {}

                        def emit_s(kt, qg=qg, h=h, s_done=s_done,
                                   m_done=m_done):
                            i = kt % 2
                            p.waits("t", G["psS"][i])
                            p.mm(lambda e, h=h, qg=qg, kt=kt, i=i: e.matmul(
                                ps[i][:, :], kT[:, h, kt * 128:(kt + 1) * 128],
                                qT[:, h, qg * 512:(qg + 1) * 512],
                                start=True, stop=True), inc=ct, k=1)
                            s_done[kt] = ct.n
                            if kt >= 4 * qg:
                                off = 384 - 128 * (kt - 4 * qg)
                                p.wait("v", ct, s_done[kt])
                                p.op("v", lambda e, i=i, off=off:
                                     e.tensor_tensor(
                                         ps[i][:, :], ps[i][:, :],
                                         mask_sb[:, off:off + 512], ALU.add),
                                     inc=cv, k=1)
                                m_done[kt] = cv.n

                        def emit_exp(kt, h=h, s_done=s_done, m_done=m_done,
                                     exp_done=exp_done):
                            i = kt % 2
                            j = kt % 3
                            if kt in m_done:
                                p.wait("a", cv, m_done[kt])
                            else:
                                p.wait("a", ct, s_done[kt])
                            p.wait("a", ca, bias_done[h])
                            p.waits("a", G["pt"][j])
                            p.op("a", lambda e, h=h, i=i, j=j: e.activation(
                                pt_sb[:, j, :], ps[i][:, :], AF.Exp,
                                bias=biasc[:, h:h + 1], scale=ISD),
                                inc=ca, k=1)
                            exp_done[kt] = ca.n
                            G["psS"][i] = gate(ca)

                        def emit_lctx(kt, h=h, nkt=nkt, cb=cb, par=par,
                                      exp_done=exp_done):
                            j = kt % 3
                            p.wait("t", ca, exp_done[kt])
                            if kt == 0:
                                p.waits("t", G["l"])
                                p.waits("t", G["ctx"][par])
                            p.mm(lambda e, kt=kt, j=j, n=nkt: e.matmul(
                                ps[3][0:1, :], onec[:, :], pt_sb[:, j, :],
                                start=(kt == 0), stop=(kt == n - 1)))
                            p.mm(lambda e, h=h, kt=kt, j=j, n=nkt, cb=cb:
                                 e.matmul(
                                     cb[:, :],
                                     v_sb[:, kt, h * 128:(h + 1) * 128],
                                     pt_sb[:, j, :],
                                     start=(kt == 0), stop=(kt == n - 1)),
                                 inc=ct, k=1)
                            G["pt"][j] = gate(ct)

                        emit_s(0)
                        emit_exp(0)
                        emit_s(1)
                        emit_exp(1)
                        if pending_epi[0] is not None:
                            pending_epi[0]()
                        for kt in range(2, nkt):
                            emit_s(kt)
                            emit_exp(kt)
                            emit_lctx(kt - 2)
                            if kt < 4 * qg:
                                dq_step()
                        emit_lctx(nkt - 2)
                        emit_lctx(nkt - 1)
                        pending_epi[0] = (
                            lambda h=h, qg=qg, cb=cb, par=par, end=ct.n:
                            emit_epi(h, qg, cb, par, end))
                        qg_idx[0] += 1

                pending_epi[0]()
                pending_epi[0] = None
                while dq_live[0]:
                    dq_step()
                p.wait("g", cg)
                p.op("g", lambda e: e.collective_compute(
                    "AllGather", ALU.bypass, replica_groups=[list(range(8))],
                    ins=[gu_contrib.ap().opt()], outs=[wdeq_gu.ap().opt()]),
                    inc=cc, k=1)
                p.op("g", lambda e: e.collective_compute(
                    "AllGather", ALU.bypass, replica_groups=[list(range(8))],
                    ins=[dn_contrib.ap().opt()], outs=[wdeq_dn.ap().opt()]),
                    inc=cc, k=1)
                spbuf = [sb(ctxB, f"spbuf{i}", [128, HID], F32)
                         for i in range(2)]
                p.wait("t", ca)
                p.wait("t", cv)

                if debug_outputs:
                    p.barrier(allc)
                    p.op("a", lambda e: e.dma_start(
                        out=dbg["ctxT"][:, :],
                        in_=ctxT[:, :, :].rearrange("q h s -> q (h s)")),
                        inc=cdbg, k=16)

                sp_g = [gate(cg), gate(cg)]
                for qt32 in range(32):
                    sbi = qt32 % 2
                    for oc in range(4):
                        pi = oc % 2
                        p.waits("t", psc_g[pi])
                        for ds in range(4):
                            p.mm(lambda e, ds=ds, qt=qt32, oc=oc, pi=pi:
                                 e.matmul(ps[pi][:, :],
                                          ctxT[:, ds, qt * 128:(qt + 1) * 128],
                                          wo_sb[:, ds, oc * 512:(oc + 1) * 512],
                                          start=(ds == 0), stop=(ds == 3)))
                        p.inc_last("t", ct)
                        p.wait("a", ct)
                        if oc == 0:
                            p.waits("a", sp_g[sbi])
                        p.op("a", lambda e, oc=oc, pi=pi, sbi=sbi: e.copy(
                            spbuf[sbi][:, oc * 512:(oc + 1) * 512],
                            ps[pi][:, :]), inc=ca, k=1)
                        psc_g[pi] = gate(ca)
                    p.wait("a", ca)
                    p.op("a", lambda e, qt=qt32, sbi=sbi: e.dma_start(
                        out=sp_in[qt * 128:(qt + 1) * 128, :],
                        in_=spbuf[sbi][:, :]), inc=c_sp[sbi], k=16)
                    sp_g[sbi] = gate(c_sp[sbi])

                p.wait("g", c_sp[0])
                p.wait("g", c_sp[1])
                p.op("g", lambda e: e.collective_compute(
                    "ReduceScatter", ALU.add, replica_groups=GROUPS,
                    ins=[sp_in.ap().opt()], outs=[s_own_d.ap().opt()]),
                    inc=cc, k=1)

                p.barrier(allc)
                with nc.named_scope("BW"), nc.Block() as blk:
                    p.flush(blk)

        with ExitStack() as ctxE:
            ident32_sb = sb(ctxE, "ident32_sb", [128, 128], F32)
            xld2 = [sb(ctxE, f"xld2_{i}", [128, HID], F32) for i in range(2)]
            p.op("y", lambda e: e.dma_start(out=ident32_sb[:, :],
                                            in_=ident32_in[:, :]), inc=cy, k=16)
            p.wait("t", cy)
            sq2 = sb(ctxE, "sq2", [128, HID], F32)
            xnbf2 = sb(ctxE, "xnbf2", [128, HID], BF16)
            xnTo = sb(ctxE, "xnTo", [128, 16, 1024], BF16)
            hT = sb(ctxE, "hT", [128, 44, 512], BF16)
            gslab = [sb(ctxE, f"gslab{i}", [128, 16, 128], BF16) for i in range(2)]
            dslab = [sb(ctxE, f"dslab{i}", [128, 44, 128], BF16) for i in range(2)]
            sg_sb = sb(ctxE, "sg_sb", [128, 512], BF16)
            ytp = sb(ctxE, "ytp", [128, 512], F32)
            ybuf = sb(ctxE, "ybuf", [128, 4, HID], F32)
            ynb = sb(ctxE, "ynb", [128, HID], F32)
            gbuf = sb(ctxE, "gbuf", [128, HID], F32)
            st_sb = sb(ctxE, "st_sb", [128, HID], F32)
            ot_sb = sb(ctxE, "ot_sb", [128, HID], F32)

            xld_gate = [[], []]
            xn_gate = []
            psTg = {0: [], 1: []}
            for gt in range(8):
                bi = gt % 2
                p.waits("y", xld_gate[bi])
                p.op("y", lambda e, b=xld2[bi], g=gt:
                     e.dma_start(out=b[:, :],
                                 in_=x_own[g * 128:(g + 1) * 128, :]),
                     inc=c_x2[bi], k=16)
                p.wait("v", c_x2[bi])
                p.op("v", lambda e, b=xld2[bi]: e.reduce_sum(
                    stat[:, 0:1], b[:, :], AX))
                p.wait("a", c_x2[bi])
                p.waits("a", xn_gate)
                p.op("a", lambda e, b=xld2[bi]: e.activation(
                    sq2[:, :], b[:, :], AF.Square, accum_out=stat[:, 7:8]),
                    inc=ca, k=1)
                p.drain("v")
                p.op("v", lambda e: e.tensor_scalar(
                    stat[:, 1:2], stat[:, 0:1], -1.0 / HID, None, ALU.mult))
                p.wait("v", ca)
                p.op("v", lambda e: e.tensor_scalar(
                    stat[:, 2:3], stat[:, 7:8], 1.0 / HID, None, ALU.mult))
                p.drain("v")
                p.op("v", lambda e: e.tensor_tensor(
                    stat[:, 3:4], stat[:, 1:2], stat[:, 1:2], ALU.mult))
                p.drain("v")
                p.op("v", lambda e: e.tensor_tensor(
                    stat[:, 4:5], stat[:, 2:3], stat[:, 3:4], ALU.subtract),
                    inc=cv, k=1)
                p.wait("a", cv)
                p.op("a", lambda e: e.activation(
                    stat[:, 5:6], stat[:, 4:5], AF.Sqrt, bias=stat[:, 12:13], scale=1.0),
                    inc=ca, k=1)
                p.wait("v", ca)
                p.op("v", lambda e: e.reciprocal(stat[:, 6:7], stat[:, 5:6]))
                p.drain("v")
                p.waits("v", xn_gate)
                p.op("v", lambda e, b=xld2[bi]: e.tensor_scalar(
                    xnbf2[:, :], b[:, :], stat[:, 1:2], stat[:, 6:7],
                    ALU.add, ALU.mult), inc=cv, k=1)
                xld_gate[bi] = gate(cv, ca)
                p.wait("t", cv)
                p.waits("t", psTg[0])
                p.waits("t", psTg[1])
                for i in range(16):
                    pst = psT if (i % 8) < 4 else psT2
                    sl = (i % 4) * 128
                    p.tr(lambda e, i=i, pst=pst, sl=sl: e.transpose(
                        pst[0:128, sl:sl + 128], xnbf2[:, i * 128:(i + 1) * 128],
                        ident_sb[:, :]))
                    if i % 8 == 7:
                        p.inc_last("t", ct)
                        p.wait("a", ct)
                        for i2 in range(i - 7, i + 1):
                            ps_ = psT if (i2 % 8) < 4 else psT2
                            sl2 = (i2 % 4) * 128
                            p.op("a", lambda e, i2=i2, ps_=ps_, sl2=sl2, g=gt:
                                 e.copy(xnTo[:, i2, g * 128:(g + 1) * 128],
                                        ps_[0:128, sl2:sl2 + 128]))
                        p.op("a", lambda e: e.drain(), inc=ca, k=1)
                        if i == 7:
                            p.wait("t", ca)
                xn_gate = gate(ca)
                psTg[0] = gate(ca)
                psTg[1] = gate(ca)
                if gt == 3:
                    ffn_xn_half = gate(ca)

            ffn_xn_gate = gate(ca)
            gsl_gate = [[], []]
            dsl_gate = [[], []]
            psg2 = {i: [] for i in range(6)}
            sg_gate = []
            hT_gate = []
            yt_gate = []
            ssq_gate = []
            ybuf_war = []

            for tg in range(2):
                for j in range(44):
                    for half, ot in ((0, j), (1, 44 + j)):
                        bi = (2 * j + half) % 2
                        qn = "y"
                        p.waits(qn, gsl_gate[bi])
                        p.wait(qn, cc, 4)
                        p.op(qn, lambda e, b=gslab[bi], o=ot: e.dma_start(
                            out=b[:, :, :], in_=wdeq_gu.ap()[o, :, :]
                            .rearrange("q (s o) -> q s o", s=16)), inc=c_gs[bi], k=16)
                        pi = half
                        p.waits("t", psg2[pi])
                        p.wait("t", c_gs[bi])
                        p.waits("t", ffn_xn_gate)
                        for su in range(16):
                            p.mm(lambda e, b=gslab[bi], su=su, tg=tg, pi=pi:
                                 e.matmul(ps[pi][:, :], b[:, su, :],
                                          xnTo[:, su, tg * 512:(tg + 1) * 512],
                                          start=(su == 0), stop=(su == 15)))
                        p.inc_last("t", ct)
                        gsl_gate[bi] = gate(ct)
                        if half == 0:
                            p.wait("a", ct)
                            p.waits("a", sg_gate)
                            p.op("a", lambda e, pi=pi: e.activation(
                                sg_sb[:, :], ps[pi][:, :], AF.Silu),
                                inc=ca, k=1)
                            psg2[pi] = gate(ca)
                        else:
                            p.wait("v", ct)
                            p.wait("v", ca)
                            p.op("v", lambda e, j=j, pi=pi: e.tensor_tensor(
                                hT[:, j, :], sg_sb[:, :], ps[pi][:, :],
                                ALU.mult), inc=cv, k=1)
                            p.drain("v")
                            sg_gate = gate(cv)
                            psg2[pi] = gate(cv)
                hT_gate = gate(cv)
                for ot in range(16):
                    bi = ot % 2
                    qn = "y"
                    p.waits(qn, dsl_gate[bi])
                    p.wait(qn, cc, 4)
                    p.op(qn, lambda e, b=dslab[bi], o=ot: e.dma_start(
                        out=b[:, :, :], in_=wdeq_dn.ap()[o, :, :]
                        .rearrange("q (s o) -> q s o", s=44)), inc=c_ds[bi], k=16)
                    pi = 4
                    p.waits("t", psg2[pi])
                    p.wait("t", c_ds[bi])
                    p.waits("t", hT_gate)
                    for su in range(44):
                        p.mm(lambda e, b=dslab[bi], su=su, pi=pi: e.matmul(
                            ps[pi][:, :], b[:, su, :], hT[:, su, :],
                            start=(su == 0), stop=(su == 43)))
                    p.inc_last("t", ct)
                    dsl_gate[bi] = gate(ct)
                    p.wait("a", ct)
                    p.waits("a", yt_gate)
                    p.op("a", lambda e, pi=pi: e.copy(ytp[:, :], ps[pi][:, :]),
                         inc=ca, k=1)
                    psg2[pi] = gate(ca)
                    p.wait("t", ca)
                    p.waits("t", psg2[5])
                    for tt in range(4):
                        p.tr(lambda e, tt=tt: e.transpose(
                            ps[5][:, tt * 128:(tt + 1) * 128],
                            ytp[:, tt * 128:(tt + 1) * 128], ident32_sb[:, :]))
                    p.inc_last("t", ct)
                    p.wait("a", ct)
                    p.waits("a", ybuf_war)
                    for tt in range(4):
                        p.op("a", lambda e, tt=tt, o=ot: e.copy(
                            ybuf[:, tt, o * 128:(o + 1) * 128],
                            ps[5][:, tt * 128:(tt + 1) * 128]))
                    p.op("a", lambda e: e.drain(), inc=ca, k=1)
                    yt_gate = gate(ca)
                    psg2[5] = gate(ca)
                p.wait("a", ca)
                p.waits("a", ssq_gate)
                for tt in range(4):
                    p.op("a", lambda e, tt=tt: e.activation(
                        sq2[:, :], ybuf[:, tt, :], AF.Square,
                        accum_out=stat[:, 24 + tt:25 + tt]))
                    p.drain("a")
                p.op("a", lambda e: e.activation(
                    stat[:, 32:36], stat[:, 24:28], AF.Sqrt, bias=stat[:, 13:14],
                    scale=1.0 / HID), inc=ca, k=1)
                p.wait("v", ca)
                p.op("v", lambda e: e.reciprocal(stat[:, 36:40], stat[:, 32:36]))
                p.drain("v")
                p.op("v", lambda e: e.engine_nop(), inc=cv, k=1)
                ssq_gate = gate(cv)
                ynb_gate = []
                gbuf_gate = []
                st_gate = []
                for tt in range(4):
                    gt = tg * 4 + tt
                    p.waits("v", ynb_gate)
                    p.op("v", lambda e, tt=tt: e.tensor_scalar(
                        ynb[:, :], ybuf[:, tt, :], stat[:, 36 + tt:37 + tt],
                        None, ALU.mult), inc=cv, k=1)
                    p.wait("a", cv)
                    p.waits("a", gbuf_gate)
                    p.op("a", lambda e: e.activation(
                        gbuf[:, :], ynb[:, :], AF.Silu), inc=ca, k=1)
                    ynb_gate = gate(ca)
                    bi = tt % 2
                    p.waits("y", xld_gate[bi])
                    p.op("y", lambda e, b=xld2[bi], g=gt: e.dma_start(
                        out=b[:, :], in_=x_own[g * 128:(g + 1) * 128, :]),
                        inc=c_x2[bi], k=16)
                    p.wait("y", cc)
                    p.waits("y", st_gate)
                    p.op("y", lambda e, g=gt: e.dma_start(
                        out=st_sb[:, :], in_=s_own_d[g * 128:(g + 1) * 128, :]),
                        inc=c_st, k=16)
                    p.wait("v", c_x2[bi]); p.wait("v", c_st)
                    p.wait("v", ca)
                    p.op("v", lambda e: e.tensor_tensor(
                        st_sb[:, :], st_sb[:, :], bo_sb[:, :], ALU.add))
                    p.drain("v")
                    p.op("v", lambda e: e.tensor_tensor(
                        st_sb[:, :], st_sb[:, :], gbuf[:, :], ALU.mult))
                    p.drain("v")
                    p.op("v", lambda e, b=xld2[bi]: e.tensor_tensor(
                        ot_sb[:, :], b[:, :], st_sb[:, :], ALU.add),
                        inc=cv, k=1)
                    xld_gate[bi] = gate(cv)
                    gbuf_gate = gate(cv)
                    st_gate = gate(cv)
                    p.wait("a", cv)
                    p.op("a", lambda e, g=gt: e.dma_start(
                        out=out_own[g * 128:(g + 1) * 128, :], in_=ot_sb[:, :]),
                        inc=cg, k=16)
                    p.wait("v", cg, cg.n)
                ssq_gate = gate(cv)
                ybuf_war = gate(cv)

            if debug_outputs:
                p.op("a", lambda e: e.dma_start(
                    out=dbg["sown"][:, :], in_=s_own_d[:, :]), inc=cdbg, k=16)

            p.wait("a", cg)
            p.barrier(allc)
            with nc.named_scope("E"), nc.Block() as blk:
                p.flush(blk)

    return nc



def _host_inputs(inputs):
    x = np.ascontiguousarray(np.asarray(inputs["L0_attn_x"], dtype=np.float32))
    wq_down = np.asarray(inputs["L0_attn_Wq_down"], np.float32)
    wq_up = np.asarray(inputs["L0_attn_Wq_up"], np.float32)
    wq_rope = np.asarray(inputs["L0_attn_Wq_rope"], np.float32)
    wkv_down = np.asarray(inputs["L0_attn_Wkv_down"], np.float32)
    wk_up = np.asarray(inputs["L0_attn_Wk_up"], np.float32)
    wk_rope = np.asarray(inputs["L0_attn_Wk_rope"], np.float32)
    wv_up = np.asarray(inputs["L0_attn_Wv_up"], np.float32)
    wo = np.asarray(inputs["L0_attn_Wo"], np.float32)
    bo = np.asarray(inputs["L0_attn_bo"], np.float32)
    gup = np.ascontiguousarray(np.asarray(inputs["L0_ffn_gate_up_packed"], np.int32))
    gus = np.asarray(inputs["L0_ffn_gate_up_scales"], np.float16)
    dnp = np.ascontiguousarray(np.asarray(inputs["L0_ffn_down_packed"], np.int32))
    dns = np.asarray(inputs["L0_ffn_down_scales"], np.float16)

    c = lambda a: np.ascontiguousarray(a)

    inv_freq = 1.0 / 10000.0 ** (np.arange(0, R, 2, dtype=np.float64) / R)
    t = np.arange(S, dtype=np.float64)
    freqs = t[:, None] * inv_freq[None, :]
    emb = np.concatenate([freqs, freqs], axis=-1)
    cos = np.cos(emb).astype(np.float32)
    sin = np.sin(emb).astype(np.float32)
    sin_adj = sin.copy()
    sin_adj[:, :16] = -sin_adj[:, :16]
    cos_t = c(np.tile(cos, (1, 4)))
    sin_t = c(np.tile(sin_adj, (1, 4)))
    ki = np.arange(128)[:, None]
    cc_ = np.arange(896)[None, :]
    mask = np.where((cc_ - 384) >= ki, 0.0, -10000.0).astype(np.float32)
    ident = np.eye(128, dtype=NPBF16)
    ident32 = np.eye(128, dtype=np.float32)
    bo_bc = c(np.tile(bo[None, :], (128, 1)))

    gus_T = np.ascontiguousarray(gus.T)
    dns_T = np.ascontiguousarray(dns.T)
    shared = {
        "wqd_t": c(wq_down.T), "wkvd_t": c(wkv_down.T),
        "wkrope_t": c(wk_rope.T), "bo_bc": bo_bc,
        "cos_t": cos_t, "sin_t": sin_t, "mask_in": mask,
        "ident_in": ident, "ident32_in": ident32,
    }
    maps = []
    for core in range(8):
        b, hg = core // 4, core % 4
        hs = slice(4 * hg * CDIM, (4 * hg + 4) * CDIM)
        rs = slice(4 * hg * R, (4 * hg + 4) * R)
        ds = slice(4 * hg * DH, (4 * hg + 4) * DH)
        m = dict(shared)
        m["xb_in"] = c(x[b])
        m["x_own"] = c(x[b, 1024 * hg:1024 * (hg + 1)])
        m["cos_own"] = c(cos_t[1024 * hg:1024 * (hg + 1)])
        m["sin_own"] = c(sin_t[1024 * hg:1024 * (hg + 1)])
        m["wqup_t"] = c(wq_up[hs].T)
        m["wkup_t"] = c(wk_up[hs].T)
        m["wqrope_t"] = c(wq_rope[rs].T)
        m["wvup_t"] = c(wv_up[ds].T)
        m["wo_t_own"] = c(wo.T[ds])
        m["gup_in"] = c(gup[core * 1408:(core + 1) * 1408])
        m["gus_t"] = c(gus_T[core * 1408:(core + 1) * 1408])
        m["dnp_in"] = c(dnp[core * 256:(core + 1) * 256])
        m["dns_t"] = c(dns_T[core * 256:(core + 1) * 256])
        maps.append(m)
    return maps


_NC_CACHE = {}


def kernel(**inputs):
    maps = _host_inputs(inputs)
    if "nc" not in _NC_CACHE:
        _NC_CACHE["nc"] = build_nc(debug_outputs=False)
    nc = _NC_CACHE["nc"]
    res = run_bass_kernel_spmd(nc, maps, core_ids=list(range(8)))
    out = np.zeros((B, S, HID), dtype=np.float32)
    for core in range(8):
        b, hg = core // 4, core % 4
        out[b, 1024 * hg:1024 * (hg + 1)] = res.results[core]["out_own"]
    return out


# version 16
# speedup vs baseline: 1.0139x; 1.0045x over previous
import math
from contextlib import ExitStack

import numpy as np
import ml_dtypes

import concourse.bass as bass
import concourse.mybir as mybir
from concourse.bass_utils import run_bass_kernel_spmd

F32 = mybir.dt.float32
F16 = mybir.dt.float16
BF16 = mybir.dt.bfloat16
I32 = mybir.dt.int32
U16 = mybir.dt.uint16
AF = mybir.ActivationFunctionType
ALU = mybir.AluOpType
AX = mybir.AxisListType.X
NPBF16 = ml_dtypes.bfloat16

B, S, HID, NH, LAT, R = 2, 4096, 2048, 16, 512, 32
DH, CDIM = 128, 96
FI = 5632
ISD = 1.0 / math.sqrt(DH)
GROUPS = [[0, 1, 2, 3], [4, 5, 6, 7]]
TWO126 = float(2.0 ** 126)
ENGINES = ("t", "v", "a", "g", "y")


class Ctr:
    def __init__(self, nc, name):
        self.h = nc.alloc_semaphore(name=name)
        self.n = 0


def gate(*ctrs):
    return [(c.h, c.n) for c in ctrs]


class Prog:
    def __init__(self, nc):
        self.nc = nc
        self.steps = {e: [] for e in ENGINES}
        self.pe_was_mm = False

    def wait(self, eng, ctr, val=None):
        v = ctr.n if val is None else val
        if v > 0:
            self.steps[eng].append(("w", ctr.h, v))

    def waits(self, eng, g):
        for (h, v) in g:
            if v > 0:
                self.steps[eng].append(("w", h, v))

    def op(self, eng, fn, inc=None, k=1):
        if inc is not None:
            inc.n += k
            self.steps[eng].append(("o", fn, inc.h, k))
        else:
            self.steps[eng].append(("o", fn, None, 0))

    def drain(self, eng):
        self.steps[eng].append(("d",))

    def inc_last(self, eng, ctr, k=1):
        for i in range(len(self.steps[eng]) - 1, -1, -1):
            st = self.steps[eng][i]
            if st[0] == "o":
                assert st[2] is None, "op already carries an inc"
                ctr.n += k
                self.steps[eng][i] = ("o", st[1], ctr.h, k)
                return
        raise AssertionError("no op to attach inc to")

    def mm(self, fn, inc=None, k=1):
        self.op("t", fn, inc, k)
        self.pe_was_mm = True

    def tr(self, fn, inc=None, k=1):
        if self.pe_was_mm:
            self.steps["t"].append(("n",))
            self.pe_was_mm = False
        self.op("t", fn, inc, k)

    def barrier(self, ctrs):
        for e in ENGINES:
            for c in ctrs:
                self.wait(e, c)

    def flush(self, block):
        emap = {"t": block.tensor, "v": block.vector, "a": block.scalar,
                "g": block.gpsimd, "y": block.sync}
        for e in ENGINES:
            steps = self.steps[e]
            if not steps:
                continue

            def body(eng, steps=steps):
                for st in steps:
                    if st[0] == "w":
                        eng.wait_ge(st[1], st[2])
                    elif st[0] == "d":
                        eng.drain()
                    elif st[0] == "n":
                        eng.nop()
                    else:
                        inst = st[1](eng)
                        if st[2] is not None:
                            inst.then_inc(st[2], st[3])
            emap[e](body)
        self.steps = {e: [] for e in ENGINES}
        self.pe_was_mm = False


def build_nc(debug_outputs=False, ablate=frozenset()):
    nc = bass.Bass("TRN2", target_bir_lowering=False,
                   dynamic_dma_scratch_size=2048)
    p = Prog(nc)

    def din(name, shape, dt):
        return nc.dram_tensor(name, list(shape), dt, kind="ExternalInput")

    xb_in = din("xb_in", [S, HID], F32)
    x_own = din("x_own", [1024, HID], F32)
    wqd_t = din("wqd_t", [HID, LAT], F32)
    wkvd_t = din("wkvd_t", [HID, LAT], F32)
    wqup_t = din("wqup_t", [LAT, 4 * CDIM], F32)
    wkup_t = din("wkup_t", [LAT, 4 * CDIM], F32)
    wqrope_t = din("wqrope_t", [LAT, 4 * R], F32)
    wkrope_t = din("wkrope_t", [HID, R], F32)
    wvup_t = din("wvup_t", [LAT, 4 * DH], F32)
    wo_t_own = din("wo_t_own", [4 * DH, HID], F32)
    bo_bc = din("bo_bc", [128, HID], F32)
    cos_t = din("cos_t", [S, 128], F32)
    sin_t = din("sin_t", [S, 128], F32)
    cos_own = din("cos_own", [1024, 128], F32)
    sin_own = din("sin_own", [1024, 128], F32)
    mask_in = din("mask_in", [128, 896], F32)
    ident_in = din("ident_in", [128, 128], BF16)
    ident32_in = din("ident32_in", [128, 128], F32)
    gup_in = din("gup_in", [11 * 128, 256], I32)
    gus_t = din("gus_t", [11 * 128, 16], F16)
    dnp_in = din("dnp_in", [2 * 128, 704], I32)
    dns_t = din("dns_t", [2 * 128, 44], F16)

    out_own = nc.dram_tensor("out_own", [1024, HID], F32, kind="ExternalOutput")

    gu_contrib = nc.dram_tensor("gu_contrib", [11, 128, 2048], BF16)
    dn_contrib = nc.dram_tensor("dn_contrib", [2, 128, 5632], BF16)
    wdeq_gu = nc.dram_tensor("wdeq_gu", [88, 128, 2048], BF16)
    wdeq_dn = nc.dram_tensor("wdeq_dn", [16, 128, 5632], BF16)
    sp_in = nc.dram_tensor("sp_in", [S, HID], F32)
    lat_c = nc.dram_tensor("lat_c", [128, 8192], BF16)
    lat_g = nc.dram_tensor("lat_g", [512, 8192], BF16)
    kr_c = nc.dram_tensor("kr_c", [32, 1024], BF16)
    kr_g = nc.dram_tensor("kr_g", [128, 1024], BF16)
    s_own_d = nc.dram_tensor("s_own_d", [1024, HID], F32)

    dbg = {}
    if debug_outputs:
        dbg["qT"] = nc.dram_tensor("dbg_qT", [128, 4 * S], BF16, kind="ExternalOutput")
        dbg["kT"] = nc.dram_tensor("dbg_kT", [128, 4 * S], BF16, kind="ExternalOutput")
        dbg["v"] = nc.dram_tensor("dbg_v", [128, 32 * 512], BF16, kind="ExternalOutput")
        dbg["ctxT"] = nc.dram_tensor("dbg_ctxT", [128, 4 * S], BF16, kind="ExternalOutput")
        dbg["sown"] = nc.dram_tensor("dbg_sown", [1024, HID], F32, kind="ExternalOutput")
        dbg["wqd"] = nc.dram_tensor("dbg_wqd", [128, 16 * 512], BF16, kind="ExternalOutput")
        dbg["xnT"] = nc.dram_tensor("dbg_xnT", [128, 16 * 512], BF16, kind="ExternalOutput")
        dbg["qlat"] = nc.dram_tensor("dbg_qlat", [128, 4 * 512], BF16, kind="ExternalOutput")
        dbg["xnT0"] = nc.dram_tensor("dbg_xnT0", [128, 16 * 512], BF16, kind="ExternalOutput")
        dbg["xld7"] = nc.dram_tensor("dbg_xld7", [128, HID], F32, kind="ExternalOutput")

    cy = Ctr(nc, "cy")
    c_w = Ctr(nc, "c_w")
    c_pk = [Ctr(nc, "c_pk0"), Ctr(nc, "c_pk1")]
    c_sc = Ctr(nc, "c_sc")
    c_x = Ctr(nc, "c_x")
    c_x2 = [Ctr(nc, "c_x2a"), Ctr(nc, "c_x2b")]
    c_cs = [Ctr(nc, "c_cs0"), Ctr(nc, "c_cs1")]
    c_gs = [Ctr(nc, "c_gs0"), Ctr(nc, "c_gs1")]
    c_ds = [Ctr(nc, "c_ds0"), Ctr(nc, "c_ds1")]
    c_st = Ctr(nc, "c_st")
    c_sp = [Ctr(nc, "c_sp0"), Ctr(nc, "c_sp1")]
    c_lat = Ctr(nc, "c_lat")
    c_kr = Ctr(nc, "c_kr")
    c_lq = [Ctr(nc, "c_lq0"), Ctr(nc, "c_lq1")]
    cg = Ctr(nc, "cg")
    cdbg = Ctr(nc, "cdbg")
    cc = Ctr(nc, "cc")
    ct = Ctr(nc, "ct")
    cv = Ctr(nc, "cv")
    ca = Ctr(nc, "ca")
    allc = [cy, c_w, c_pk[0], c_pk[1], c_sc, c_x, c_x2[0], c_x2[1], c_cs[0], c_cs[1],
            c_gs[0], c_gs[1], c_ds[0], c_ds[1], c_st, c_sp[0], c_sp[1],
            c_lat, c_kr, c_lq[0], c_lq[1], cg, cdbg, ct, cv, ca]

    def sb(ctx, name, shape, dt):
        return ctx.enter_context(nc.sbuf_tensor(name, list(shape), dt))

    with ExitStack() as ctx0:
        ps = [ctx0.enter_context(nc.psum_tensor(f"ps{i}", [128, 512], F32))
              for i in range(6)]
        psT = ctx0.enter_context(nc.psum_tensor("psT", [128, 512], BF16))
        psT2 = ctx0.enter_context(nc.psum_tensor("psT2", [128, 512], BF16))

        ident_sb = sb(ctx0, "ident_sb", [128, 128], BF16)
        bo_sb = sb(ctx0, "bo_sb", [128, HID], F32)
        stat = sb(ctx0, "stat", [128, 64], F32)

        wspec = [("wqd", wqd_t, 16, LAT), ("wkvd", wkvd_t, 16, LAT),
                 ("wqup", wqup_t, 4, 4 * CDIM), ("wkup", wkup_t, 4, 4 * CDIM),
                 ("wqrope", wqrope_t, 4, 4 * R), ("wkrope", wkrope_t, 16, R),
                 ("wvup", wvup_t, 4, 4 * DH)]

        with ExitStack() as ctxAB:
            qT = sb(ctxAB, "qT", [128, 4, S], BF16)
            kT = sb(ctxAB, "kT", [128, 4, S], BF16)
            v_sb = sb(ctxAB, "v_sb", [128, 32, 512], BF16)
            wo_sb = sb(ctxAB, "wo_sb", [128, 4, HID], BF16)

            scrD = sb(ctxAB, "scrD", [128, 4608], F32)
            dq_bits = scrD[:, 0:1024].bitcast(U16)
            dq_sh = scrD[:, 1024:1280].bitcast(U16)
            dq_hi = scrD[:, 1280:1536].bitcast(U16)
            dq_lo = scrD[:, 1536:1792].bitcast(U16)
            dq_pk = [scrD[:, 1792:2048].bitcast(I32),
                     scrD[:, 2048:2304].bitcast(I32)]
            dq_sc = scrD[:, 2304:2312].bitcast(F16)
            dq_s32 = scrD[:, 2320:2336]
            dq_wscl = scrD[:, 2336:3360].bitcast(BF16).rearrange(
                "q (g o) -> q g o", g=16)
            dq_wob = scrD[:, 3360:4384].bitcast(BF16).rearrange(
                "q (g o) -> q g o", g=16)

            ctxWA = ExitStack()
            wsb = {}
            for name, src, nsub, m in wspec:
                wsb[name] = sb(ctxWA, name + "_sb", [128, nsub, m], BF16)
            wsb["wo"] = wo_sb

            with ExitStack() as ctxP0:
                scr0 = sb(ctxP0, "scr0", [128, 8192], F32)
                wf32 = scr0[:, 0:8192]

                for dst, src in ((ident_sb, ident_in),
                                 (bo_sb, bo_bc)):
                    p.op("y", lambda e, d=dst, s_=src:
                         e.dma_start(out=d[:, :], in_=s_[:, :]), inc=c_w, k=16)

                p.op("v", lambda e: e.memset(stat[:, 12:13], 1e-5))
                p.op("v", lambda e: e.memset(stat[:, 13:14], 1e-6))
                p.drain("v")

                cg_gate = []
                for name, src, nsub, m in (wspec + [("wo", wo_t_own, 4, HID)]):
                    dst = wsb[name]
                    ap_in = src.ap().rearrange("(s q) m -> q s m", q=128)
                    p.waits("y", cg_gate)
                    p.op("y", lambda e, a=ap_in, n=nsub, mm_=m:
                         e.dma_start(out=wf32[:, 0:n * mm_].rearrange(
                             "q (s m) -> q s m", s=n), in_=a), inc=c_w, k=16)
                    p.wait("v", c_w)
                    p.op("v", lambda e, d=dst, n=nsub, mm_=m:
                         e.tensor_copy(d[:, :, :], wf32[:, 0:n * mm_].rearrange(
                             "q (s m) -> q s m", s=n)), inc=cv, k=1)
                    p.drain("v")
                    cg_gate = gate(cv)

                p.barrier(allc)
                with nc.named_scope("P0"), nc.Block() as blk:
                    p.flush(blk)

            with ExitStack() as ctxA:
                cosb = sb(ctxA, "cosb", [128, 2, 128], F32)
                sinb = sb(ctxA, "sinb", [128, 2, 128], F32)
                rsc = sb(ctxA, "rsc", [128, 128], F32)
                rsw = sb(ctxA, "rsw", [128, 128], F32)
                ropebf = sb(ctxA, "ropebf", [128, 128], BF16)

                rsc_gate = []
                ropebf_gate = []
                cs_gate = [[], []]
                psg = {i: [] for i in range(6)}
                psTg = []
                psT2g = []

                with ExitStack() as ctxOwn:
                    xld = sb(ctxOwn, "xld0", [128, HID], F32)
                    xnbf = sb(ctxOwn, "xnbf", [128, HID], BF16)
                    xnh = sb(ctxOwn, "xnh", [128, 16, 512], BF16)
                    latst = sb(ctxOwn, "latst", [128, 4, 512], BF16)
                    krT_sb = sb(ctxOwn, "krT_sb", [32, 8, 128], BF16)

                    xld_gate = []
                    xn_gate = []
                    xnh_war = []
                    latst_gate = []
                    for half in range(2):
                        for tt in range(4):
                            gt = half * 4 + tt
                            p.waits("y", xld_gate)
                            p.op("y", lambda e, g=gt:
                                 e.dma_start(out=xld[:, :],
                                             in_=x_own[g * 128:(g + 1) * 128, :]),
                                 inc=c_x, k=16)
                            p.wait("v", c_x)
                            for sg4 in range(4):
                                p.op("v", lambda e, sg4=sg4: e.bn_stats(
                                    stat[:, 16 + 6 * sg4:22 + 6 * sg4],
                                    xld[:, 512 * sg4:512 * (sg4 + 1)]))
                                p.drain("v")
                            p.op("v", lambda e: e.bn_aggr(
                                stat[:, 0:2], stat[:, 16:40].rearrange(
                                    "q (g s) -> q g s", g=4)))
                            p.drain("v")
                            p.op("v", lambda e: e.tensor_scalar(
                                stat[:, 2:3], stat[:, 0:1], -1.0, None,
                                ALU.mult), inc=cv, k=1)
                            p.wait("a", cv)
                            p.op("a", lambda e: e.activation(
                                stat[:, 5:6], stat[:, 1:2], AF.Sqrt,
                                bias=stat[:, 12:13], scale=1.0), inc=ca, k=1)
                            p.wait("v", ca)
                            p.op("v", lambda e: e.reciprocal(
                                stat[:, 6:7], stat[:, 5:6]))
                            p.drain("v")
                            p.waits("v", xn_gate)
                            p.op("v", lambda e: e.tensor_scalar(
                                xnbf[:, :], xld[:, :], stat[:, 2:3],
                                stat[:, 6:7], ALU.add, ALU.mult), inc=cv, k=1)
                            xld_gate = gate(cv)
                            p.wait("t", cv)
                            for b4 in range(4):
                                pst = psT if b4 % 2 == 0 else psT2
                                if b4 % 2 == 0:
                                    p.waits("t", psTg)
                                else:
                                    p.waits("t", psT2g)
                                for i in range(b4 * 4, b4 * 4 + 4):
                                    sl = (i % 4) * 128
                                    p.tr(lambda e, i=i, sl=sl, pst=pst:
                                         e.transpose(
                                             pst[0:128, sl:sl + 128],
                                             xnbf[:, i * 128:(i + 1) * 128],
                                             ident_sb[:, :]))
                                p.inc_last("t", ct)
                                p.wait("a", ct)
                                if tt == 0 and b4 == 0:
                                    p.waits("a", xnh_war)
                                for i in range(b4 * 4, b4 * 4 + 4):
                                    sl = (i % 4) * 128
                                    p.op("a", lambda e, i=i, sl=sl, t=tt,
                                         pst=pst:
                                         e.copy(xnh[:, i, t * 128:(t + 1) * 128],
                                                pst[0:128, sl:sl + 128]))
                                p.op("a", lambda e: e.drain(), inc=ca, k=1)
                                if b4 % 2 == 0:
                                    psTg = gate(ca)
                                else:
                                    psT2g = gate(ca)
                            xn_gate = gate(ca)
                        xnh_gate = gate(ca)

                        p.waits("t", xnh_gate)
                        for wi, wname in ((0, "wqd"), (1, "wkvd")):
                            for lt in range(4):
                                pi = lt % 2
                                p.waits("t", psg[pi])
                                for su in range(16):
                                    p.mm(lambda e, w=wsb[wname], su=su, lt=lt,
                                         pi=pi: e.matmul(
                                             ps[pi][:, :],
                                             w[:, su, lt * 128:(lt + 1) * 128],
                                             xnh[:, su, :], start=(su == 0),
                                             stop=(su == 15)))
                                p.inc_last("t", ct)
                                p.wait("a", ct)
                                if lt == 0:
                                    p.waits("a", latst_gate)
                                p.op("a", lambda e, lt=lt, pi=pi: e.copy(
                                    latst[:, lt, :], ps[pi][:, :]), inc=ca, k=1)
                                psg[pi] = gate(ca)
                            p.wait("a", ca)
                            p.op("a", lambda e, wi=wi, half=half: e.dma_start(
                                out=lat_c.ap()[:, wi * 4096:(wi + 1) * 4096]
                                .rearrange("p (lt t) -> p lt t", lt=4)
                                [:, :, half * 512:(half + 1) * 512],
                                in_=latst[:, :, :]), inc=c_lat, k=16)
                            latst_gate = gate(c_lat)

                        for tt in range(4):
                            t8 = half * 4 + tt
                            cslot = t8 % 2
                            p.waits("y", cs_gate[cslot])
                            p.op("y", lambda e, g=t8, t=cslot:
                                 e.dma_start(out=cosb[:, t, :],
                                             in_=cos_own[g * 128:(g + 1) * 128, :]),
                                 inc=c_cs[cslot], k=16)
                            p.op("y", lambda e, g=t8, t=cslot:
                                 e.dma_start(out=sinb[:, t, :],
                                             in_=sin_own[g * 128:(g + 1) * 128, :]),
                                 inc=c_cs[cslot], k=16)
                            pi = 5
                            p.waits("t", psg[pi])
                            for su in range(16):
                                p.mm(lambda e, su=su, t=tt, pi=pi:
                                     e.matmul(
                                         ps[pi][:, 0:R],
                                         xnh[:, su, t * 128:(t + 1) * 128],
                                         wsb["wkrope"][:, su, :],
                                         start=(su == 0), stop=(su == 15)))
                            p.inc_last("t", ct)
                            p.wait("a", ct)
                            p.waits("a", rsc_gate)
                            p.op("a", lambda e, pi=pi: e.copy(
                                rsc[:, 0:R], ps[pi][:, 0:R]), inc=ca, k=1)
                            psg[pi] = gate(ca)
                            p.wait("v", ca)
                            p.wait("v", c_cs[cslot])
                            p.op("v", lambda e: e.tensor_copy(
                                rsw[:, 0:16], rsc[:, 16:32]))
                            p.op("v", lambda e: e.tensor_copy(
                                rsw[:, 16:32], rsc[:, 0:16]))
                            p.drain("v")
                            p.op("v", lambda e, t=cslot: e.tensor_tensor(
                                rsc[:, 0:R], rsc[:, 0:R], cosb[:, t, 0:R],
                                ALU.mult))
                            p.op("v", lambda e, t=cslot: e.tensor_tensor(
                                rsw[:, 0:R], rsw[:, 0:R], sinb[:, t, 0:R],
                                ALU.mult))
                            p.drain("v")
                            p.waits("v", ropebf_gate)
                            p.op("v", lambda e: e.tensor_tensor(
                                ropebf[:, 0:R], rsc[:, 0:R], rsw[:, 0:R],
                                ALU.add), inc=cv, k=1)
                            rsc_gate = gate(cv)
                            cs_gate[cslot] = gate(cv)
                            p.wait("t", cv)
                            p.waits("t", psT2g)
                            p.tr(lambda e: e.transpose(
                                psT2[0:R, 0:128], ropebf[:, 0:R],
                                ident_sb[:, :]), inc=ct, k=1)
                            ropebf_gate = gate(ct)
                            p.wait("a", ct)
                            p.op("a", lambda e, g=t8: e.copy(
                                krT_sb[:, g, :], psT2[0:R, 0:128]))
                            p.op("a", lambda e: e.drain(), inc=ca, k=1)
                            psT2g = gate(ca)
                        xnh_war = gate(ct)
                    p.wait("a", ca)
                    p.op("a", lambda e: e.dma_start(
                        out=kr_c.ap()[:, :],
                        in_=krT_sb[:, :, :].rearrange("p a b -> p (a b)")),
                        inc=c_lat, k=16)

                    p.wait("g", c_lat)
                    p.op("g", lambda e: e.collective_compute(
                        "AllGather", ALU.bypass, replica_groups=GROUPS,
                        ins=[lat_c.ap().opt()], outs=[lat_g.ap().opt()]),
                        inc=cc, k=1)
                    p.op("g", lambda e: e.collective_compute(
                        "AllGather", ALU.bypass, replica_groups=GROUPS,
                        ins=[kr_c.ap().opt()], outs=[kr_g.ap().opt()]),
                        inc=cc, k=1)

                p.wait("y", cc, 2)
                for c4 in range(4):
                    for h in range(4):
                        p.op("y", lambda e, c4=c4, h=h: e.dma_start(
                            out=kT[CDIM:128, h, c4 * 1024:(c4 + 1) * 1024],
                            in_=kr_g.ap()[c4 * 32:(c4 + 1) * 32, :]),
                            inc=c_kr, k=16)


                def dequant_gen():
                    pk_gate = [[], []]
                    sc_gate = []
                    dec_gate = []
                    wob_gate = []
                    units = [("gu", ot, 0) for ot in range(11)] +                             [("dn", ot, qq) for ot in range(2)
                             for qq in range(4)]
                    for ui, (kind, ot, qq) in enumerate(units):
                        Wu = 256 if kind == "gu" else 176
                        U = 2 * Wu
                        Gd = Wu // 16
                        srcp = gup_in if kind == "gu" else dnp_in
                        ssrc = gus_t if kind == "gu" else dns_t
                        ddst = gu_contrib if kind == "gu" else dn_contrib
                        buf = dq_pk[ui % 2]
                        p.waits("y", pk_gate[ui % 2])
                        p.op("y", lambda e, b=buf, s_=srcp, o=ot, q=qq, w=Wu:
                             e.dma_start(out=b[:, 0:w],
                                         in_=s_[o * 128:(o + 1) * 128,
                                                q * w:(q + 1) * w]),
                             inc=c_pk[ui % 2], k=16)
                        p.waits("y", sc_gate)
                        p.op("y", lambda e, s_=ssrc, o=ot, q=qq, g=Gd:
                             e.dma_start(out=dq_sc[:, 0:g],
                                         in_=s_[o * 128:(o + 1) * 128,
                                                q * g:(q + 1) * g]),
                             inc=c_sc, k=16)
                        yield
                        p.wait("v", c_pk[ui % 2])
                        p.wait("v", c_sc)
                        p.waits("v", dec_gate)
                        p.op("v", lambda e, g=Gd: e.tensor_scalar(
                            dq_s32[:, 0:g], dq_sc[:, 0:g], TWO126, None,
                            ALU.mult))
                        u16v = buf[:, 0:Wu].bitcast(U16)
                        for j in range(4):
                            sv = u16v
                            if j > 0:
                                p.drain("v")
                                p.op("v", lambda e, u=u16v, jj=j, UU=U:
                                     e.tensor_scalar(
                                         dq_sh[:, 0:UU], u, 4 * jj, None,
                                         ALU.logical_shift_right))
                                p.drain("v")
                                sv = dq_sh[:, 0:U]
                            p.op("v", lambda e, s_=sv, UU=U: e.tensor_scalar(
                                dq_hi[:, 0:UU], s_, 8, 12, ALU.bitwise_and,
                                ALU.logical_shift_left))
                            p.op("v", lambda e, s_=sv, UU=U: e.tensor_scalar(
                                dq_lo[:, 0:UU], s_, 7, 6, ALU.bitwise_and,
                                ALU.logical_shift_left))
                            p.drain("v")
                            p.op("v", lambda e, jj=j, UU=U: e.tensor_tensor(
                                dq_bits[:, 0:4 * UU].rearrange(
                                    "q (u j) -> q u j", j=4)[:, :, jj],
                                dq_hi[:, 0:UU], dq_lo[:, 0:UU],
                                ALU.bitwise_or))
                            yield
                        p.drain("v")
                        bview = dq_bits[:, 0:4 * U].bitcast(BF16)
                        for g in range(Gd):
                            p.op("v", lambda e, g=g, bv=bview: e.tensor_scalar(
                                dq_wscl[:, g, :], bv[:, 128 * g:128 * (g + 1)],
                                dq_s32[:, g:g + 1], None, ALU.mult))
                            if g % 6 == 5:
                                yield
                        p.drain("v")
                        p.op("v", lambda e: e.engine_nop(), inc=cv, k=1)
                        wsc_cv = cv.n
                        pk_gate[ui % 2] = gate(cv)
                        sc_gate = gate(cv)
                        yield
                        p.wait("t", cv, wsc_cv)
                        for g0 in range(0, Gd, 4):
                            gges = list(range(g0, min(g0 + 4, Gd)))
                            p.wait("t", ca)
                            for g in gges:
                                sl = (g % 4) * 128
                                p.tr(lambda e, g=g, sl=sl: e.transpose(
                                    psT[0:128, sl:sl + 128], dq_wscl[:, g, :],
                                    ident_sb[:, :]))
                            p.inc_last("t", ct)
                            p.wait("a", ct)
                            if g0 == 0:
                                p.waits("a", wob_gate)
                            for g in gges:
                                sl = (g % 4) * 128
                                p.op("a", lambda e, g=g, sl=sl: e.copy(
                                    dq_wob[:, g, :], psT[0:128, sl:sl + 128]))
                            p.op("a", lambda e: e.drain(), inc=ca, k=1)
                            yield
                        dec_gate = gate(ct)
                        p.wait("a", ca)
                        p.op("a", lambda e, d=ddst, o=ot, q=qq, g=Gd:
                             e.dma_start(
                                 out=d.ap()[o, :,
                                            q * g * 128:(q + 1) * g * 128],
                                 in_=dq_wob[:, 0:g, :].rearrange(
                                     "q g o -> q (g o)")),
                             inc=cg, k=16)
                        wob_gate = gate(cg)
                        yield

                dq_gen = dequant_gen()
                dq_live = [True]

                def dq_step():
                    if dq_live[0]:
                        try:
                            next(dq_gen)
                        except StopIteration:
                            dq_live[0] = False

                for _ in range(3):
                    dq_step()

                qlatb = [sb(ctxA, f"qlatb{i}", [128, 4, 512], BF16)
                         for i in range(2)]
                kvlatb = [sb(ctxA, f"kvlatb{i}", [128, 4, 512], BF16)
                          for i in range(2)]
                lq_gate = [[], []]
                for ck in range(8):
                    bi = ck % 2
                    c4 = ck // 2
                    toff = (ck % 2) * 512
                    p.waits("y", lq_gate[bi])
                    p.wait("y", cc, 1)
                    p.op("y", lambda e, b=qlatb[bi], c4=c4, to=toff:
                         e.dma_start(out=b[:, :, :],
                                     in_=lat_g.ap()[c4 * 128:(c4 + 1) * 128,
                                                    0:4096]
                                     .rearrange("p (lt t) -> p lt t", lt=4)
                                     [:, :, to:to + 512]), inc=c_lq[bi], k=16)
                    p.op("y", lambda e, b=kvlatb[bi], c4=c4, to=toff:
                         e.dma_start(out=b[:, :, :],
                                     in_=lat_g.ap()[c4 * 128:(c4 + 1) * 128,
                                                    4096:8192]
                                     .rearrange("p (lt t) -> p lt t", lt=4)
                                     [:, :, to:to + 512]), inc=c_lq[bi], k=16)

                    p.wait("t", c_lq[bi])
                    for (wname, latb, dstT) in (("wkup", kvlatb[bi], kT),
                                                ("wqup", qlatb[bi], qT)):
                        for h in range(4):
                            pi = 2 + (h % 2)
                            p.waits("t", psg[pi])
                            for ls in range(4):
                                p.mm(lambda e, w=wsb[wname], ls=ls, h=h,
                                     la=latb, pi=pi: e.matmul(
                                         ps[pi][0:CDIM, :],
                                         w[:, ls, h * CDIM:(h + 1) * CDIM],
                                         la[:, ls, :], start=(ls == 0),
                                         stop=(ls == 3)))
                            p.inc_last("t", ct)
                            p.wait("a", ct)
                            p.op("a", lambda e, d=dstT, h=h, c=ck, pi=pi:
                                 e.copy(d[0:CDIM, h, c * 512:(c + 1) * 512],
                                        ps[pi][0:CDIM, :]), inc=ca, k=1)
                            psg[pi] = gate(ca)
                            if h % 2 == 1:
                                dq_step()

                    for tt in range(4):
                        gt = ck * 4 + tt
                        pi = 4
                        p.waits("t", psg[pi])
                        for ls in range(4):
                            p.mm(lambda e, ls=ls, t=tt, pi=pi, la=kvlatb[bi]:
                                 e.matmul(
                                     ps[pi][:, :],
                                     la[:, ls, t * 128:(t + 1) * 128],
                                     wsb["wvup"][:, ls, :], start=(ls == 0),
                                     stop=(ls == 3)))
                        p.inc_last("t", ct)
                        p.wait("v", ct)
                        p.op("v", lambda e, g=gt, pi=pi: e.tensor_copy(
                            v_sb[:, g, :], ps[pi][:, :]), inc=cv, k=1)
                        p.drain("v")
                        psg[pi] = gate(cv)
                        if tt % 2 == 1:
                            dq_step()

                    for tt in range(4):
                        gt = ck * 4 + tt
                        cslot = gt % 2
                        p.waits("y", cs_gate[cslot])
                        p.op("y", lambda e, g=gt, t=cslot:
                             e.dma_start(out=cosb[:, t, :],
                                         in_=cos_t[g * 128:(g + 1) * 128, :]),
                             inc=c_cs[cslot], k=16)
                        p.op("y", lambda e, g=gt, t=cslot:
                             e.dma_start(out=sinb[:, t, :],
                                         in_=sin_t[g * 128:(g + 1) * 128, :]),
                             inc=c_cs[cslot], k=16)
                        pi = 5
                        p.waits("t", psg[pi])
                        for ls in range(4):
                            p.mm(lambda e, ls=ls, t=tt, pi=pi, la=qlatb[bi]:
                                 e.matmul(
                                     ps[pi][:, 0:128],
                                     la[:, ls, t * 128:(t + 1) * 128],
                                     wsb["wqrope"][:, ls, :], start=(ls == 0),
                                     stop=(ls == 3)))
                        p.inc_last("t", ct)
                        p.wait("a", ct)
                        p.waits("a", rsc_gate)
                        p.op("a", lambda e, pi=pi: e.copy(
                            rsc[:, :], ps[pi][:, 0:128]), inc=ca, k=1)
                        psg[pi] = gate(ca)
                        p.wait("v", ca)
                        p.wait("v", c_cs[cslot])
                        p.op("v", lambda e: e.tensor_copy(
                            rsw[:, :].rearrange("q (h r) -> q h r", h=4)[:, :, 0:16],
                            rsc[:, :].rearrange("q (h r) -> q h r", h=4)[:, :, 16:32]))
                        p.op("v", lambda e: e.tensor_copy(
                            rsw[:, :].rearrange("q (h r) -> q h r", h=4)[:, :, 16:32],
                            rsc[:, :].rearrange("q (h r) -> q h r", h=4)[:, :, 0:16]))
                        p.drain("v")
                        p.op("v", lambda e, t=cslot: e.tensor_tensor(
                            rsc[:, :], rsc[:, :], cosb[:, t, :], ALU.mult))
                        p.op("v", lambda e, t=cslot: e.tensor_tensor(
                            rsw[:, :], rsw[:, :], sinb[:, t, :], ALU.mult))
                        p.drain("v")
                        p.waits("v", ropebf_gate)
                        p.op("v", lambda e: e.tensor_tensor(
                            ropebf[:, :], rsc[:, :], rsw[:, :], ALU.add),
                            inc=cv, k=1)
                        rsc_gate = gate(cv)
                        cs_gate[cslot] = gate(cv)
                        p.wait("t", cv)
                        p.waits("t", psT2g)
                        for h in range(4):
                            p.tr(lambda e, h=h: e.transpose(
                                psT2[0:R, h * 128:(h + 1) * 128],
                                ropebf[:, h * R:(h + 1) * R], ident_sb[:, :]))
                        p.inc_last("t", ct)
                        ropebf_gate = gate(ct)
                        p.wait("a", ct)
                        for h in range(4):
                            p.op("a", lambda e, h=h, g=gt: e.copy(
                                qT[CDIM:128, h, g * 128:(g + 1) * 128],
                                psT2[0:R, h * 128:(h + 1) * 128]))
                        p.op("a", lambda e: e.drain(), inc=ca, k=1)
                        psT2g = gate(ca)
                    lq_gate[bi] = gate(ct)
                p.wait("t", c_kr)

                if debug_outputs:
                    p.barrier(allc)
                    p.op("a", lambda e: e.dma_start(
                        out=dbg["wqd"][:, :],
                        in_=wsb["wqd"][:, :, :].rearrange("q h s -> q (h s)")),
                        inc=cdbg, k=16)
                    p.op("a", lambda e: e.dma_start(
                        out=dbg["qT"][:, :],
                        in_=qT[:, :, :].rearrange("q h s -> q (h s)")),
                        inc=cdbg, k=16)
                    p.op("a", lambda e: e.dma_start(
                        out=dbg["kT"][:, :],
                        in_=kT[:, :, :].rearrange("q h s -> q (h s)")),
                        inc=cdbg, k=16)
                    p.op("a", lambda e: e.dma_start(
                        out=dbg["v"][:, :],
                        in_=v_sb[:, :, :].rearrange("q h s -> q (h s)")),
                        inc=cdbg, k=16)

                p.barrier(allc)
                with nc.named_scope("A"), nc.Block() as blk:
                    p.flush(blk)

            ctxWA.close()

            with ExitStack() as ctxB:
                mask_sb = sb(ctxB, "mask_sb", [128, 896], F32)
                ctxT = sb(ctxB, "ctxT", [128, 4, S], BF16)
                p.op("y", lambda e: e.dma_start(out=mask_sb[:, :],
                                                in_=mask_in[:, :]),
                     inc=cy, k=16)
                p.wait("v", cy)
                pt_sb = sb(ctxB, "pt_sb", [128, 3, 512], BF16)
                sqb = sb(ctxB, "sqb", [128, 2, 512], BF16)
                nrm = sb(ctxB, "nrm", [1, 64], F32)
                onec = sb(ctxB, "onec", [128, 1], BF16)
                oner = sb(ctxB, "oner", [1, 128], F32)
                biasc = sb(ctxB, "biasc", [128, 4], F32)
                linvr = sb(ctxB, "linvr", [1, 512], F32)
                linvb = sb(ctxB, "linvb", [128, 512], F32)
                p.op("v", lambda e: e.memset(onec[:, :], 1.0))
                p.op("v", lambda e: e.memset(oner[:, :], 1.0))
                p.drain("v")
                p.op("v", lambda e: e.engine_nop(), inc=cv, k=1)
                ones_gate = gate(cv)


                G = {"psS": [[], []],
                     "pt": [[], [], []],
                     "sq": [[], []],
                     "ps5": [],
                     "l": [],
                     "ctx": [[], []],
                     "nrm": [],
                     "nrm2": []}
                bias_done = [0, 0, 0, 0]
                psc_g = {0: [], 1: [], 2: [], 3: []}
                pending_epi = [None]
                qg_idx = [0]

                def emit_epi(h, qg, cb, par, end_ct):
                    p.wait("v", ct, end_ct)
                    p.op("v", lambda e: e.reciprocal(
                        linvr[0:1, :], ps[3][0:1, :]), inc=cv, k=1)
                    G["l"] = gate(cv)
                    p.wait("t", cv)
                    p.waits("t", G["ps5"])
                    p.mm(lambda e: e.matmul(
                        ps[5][:, :], oner[:, :], linvr[0:1, :],
                        start=True, stop=True), inc=ct, k=1)
                    p.wait("a", ct)
                    p.op("a", lambda e: e.copy(linvb[:, :], ps[5][:, :]),
                         inc=ca, k=1)
                    G["ps5"] = gate(ca)
                    p.wait("v", ca)
                    p.op("v", lambda e, h=h, qg=qg, cb=cb: e.tensor_tensor(
                        ctxT[:, h, qg * 512:(qg + 1) * 512],
                        cb[:, :], linvb[:, :], ALU.mult), inc=cv, k=1)
                    G["ctx"][par] = gate(cv)

                for h in range(4):
                    if pending_epi[0] is not None:
                        pending_epi[0]()
                        pending_epi[0] = None
                    rd_done = {}
                    for idx in range(16):
                        src_i, t8 = idx // 8, idx % 8
                        srcT = qT if src_i == 0 else kT
                        slot = idx % 2
                        row = 64 if (idx % 2) else 0
                        p.waits("v", G["sq"][slot])
                        p.waits("v", G["nrm"])
                        p.op("v", lambda e, h=h, s=srcT, t8=t8, slot=slot:
                             e.tensor_tensor(
                                 sqb[:, slot, :],
                                 s[:, h, t8 * 512:(t8 + 1) * 512],
                                 s[:, h, t8 * 512:(t8 + 1) * 512], ALU.mult),
                             inc=cv, k=1)
                        p.wait("t", cv)
                        if idx == 0:
                            p.waits("t", G["ps5"])
                            if h == 0:
                                p.waits("t", ones_gate)
                        if idx >= 2:
                            p.wait("t", cv, rd_done[idx - 2])
                        p.mm(lambda e, slot=slot, row=row: e.matmul(
                            ps[5][row:row + 1, :], onec[:, :],
                            sqb[:, slot, :],
                            start=True, stop=True), inc=ct, k=1)
                        G["sq"][slot] = gate(ct)
                        p.wait("v", ct)
                        p.op("v", lambda e, idx=idx, row=row: e.reduce_max(
                            nrm[0:1, idx:idx + 1],
                            ps[5][row:row + 1, :], AX), inc=cv, k=1)
                        rd_done[idx] = cv.n
                    G["ps5"] = gate(cv)
                    p.waits("v", G["nrm2"])
                    p.op("v", lambda e: e.reduce_max(
                        nrm[0:1, 16:17], nrm[0:1, 0:8], AX))
                    p.drain("v")
                    p.op("v", lambda e: e.reduce_max(
                        nrm[0:1, 17:18], nrm[0:1, 8:16], AX), inc=cv, k=1)
                    p.wait("a", cv)
                    p.op("a", lambda e: e.activation(
                        nrm[0:1, 20:22], nrm[0:1, 16:18], AF.Sqrt),
                        inc=ca, k=1)
                    G["nrm"] = gate(ca)
                    p.wait("v", ca)
                    p.op("v", lambda e: e.tensor_tensor(
                        nrm[0:1, 24:25], nrm[0:1, 20:21], nrm[0:1, 21:22],
                        ALU.mult))
                    p.drain("v")
                    p.op("v", lambda e: e.tensor_scalar(
                        nrm[0:1, 25:26], nrm[0:1, 24:25], -ISD, -2.0,
                        ALU.mult, ALU.add), inc=cv, k=1)
                    p.wait("t", cv)
                    p.waits("t", G["ps5"])
                    p.mm(lambda e: e.matmul(ps[5][0:128, 0:1], oner[:, :],
                                            nrm[0:1, 25:26],
                                            start=True, stop=True), inc=ct, k=1)
                    G["nrm2"] = gate(ct)
                    p.wait("a", ct)
                    p.op("a", lambda e, h=h: e.copy(
                        biasc[:, h:h + 1], ps[5][:, 0:1]), inc=ca, k=1)
                    bias_done[h] = ca.n
                    G["ps5"] = gate(ca)

                    for qg in range(8):
                        nkt = 4 * qg + 4
                        par = qg_idx[0] % 2
                        cb = ps[4] if par == 0 else ps[2]
                        s_done = {}
                        m_done = {}
                        exp_done = {}

                        def emit_s(kt, qg=qg, h=h, s_done=s_done,
                                   m_done=m_done):
                            i = kt % 2
                            p.waits("t", G["psS"][i])
                            p.mm(lambda e, h=h, qg=qg, kt=kt, i=i: e.matmul(
                                ps[i][:, :], kT[:, h, kt * 128:(kt + 1) * 128],
                                qT[:, h, qg * 512:(qg + 1) * 512],
                                start=True, stop=True), inc=ct, k=1)
                            s_done[kt] = ct.n
                            if kt >= 4 * qg:
                                off = 384 - 128 * (kt - 4 * qg)
                                p.wait("v", ct, s_done[kt])
                                p.op("v", lambda e, i=i, off=off:
                                     e.tensor_tensor(
                                         ps[i][:, :], ps[i][:, :],
                                         mask_sb[:, off:off + 512], ALU.add),
                                     inc=cv, k=1)
                                m_done[kt] = cv.n

                        def emit_exp(kt, h=h, s_done=s_done, m_done=m_done,
                                     exp_done=exp_done):
                            i = kt % 2
                            j = kt % 3
                            if kt in m_done:
                                p.wait("a", cv, m_done[kt])
                            else:
                                p.wait("a", ct, s_done[kt])
                            p.wait("a", ca, bias_done[h])
                            p.waits("a", G["pt"][j])
                            p.op("a", lambda e, h=h, i=i, j=j: e.activation(
                                pt_sb[:, j, :], ps[i][:, :], AF.Exp,
                                bias=biasc[:, h:h + 1], scale=ISD),
                                inc=ca, k=1)
                            exp_done[kt] = ca.n
                            G["psS"][i] = gate(ca)

                        def emit_lctx(kt, h=h, nkt=nkt, cb=cb, par=par,
                                      exp_done=exp_done):
                            j = kt % 3
                            p.wait("t", ca, exp_done[kt])
                            if kt == 0:
                                p.waits("t", G["l"])
                                p.waits("t", G["ctx"][par])
                            p.mm(lambda e, kt=kt, j=j, n=nkt: e.matmul(
                                ps[3][0:1, :], onec[:, :], pt_sb[:, j, :],
                                start=(kt == 0), stop=(kt == n - 1)))
                            p.mm(lambda e, h=h, kt=kt, j=j, n=nkt, cb=cb:
                                 e.matmul(
                                     cb[:, :],
                                     v_sb[:, kt, h * 128:(h + 1) * 128],
                                     pt_sb[:, j, :],
                                     start=(kt == 0), stop=(kt == n - 1)),
                                 inc=ct, k=1)
                            G["pt"][j] = gate(ct)

                        emit_s(0)
                        emit_exp(0)
                        emit_s(1)
                        emit_exp(1)
                        if pending_epi[0] is not None:
                            pending_epi[0]()
                        for kt in range(2, nkt):
                            emit_s(kt)
                            emit_exp(kt)
                            emit_lctx(kt - 2)
                            if kt < 4 * qg:
                                dq_step()
                        emit_lctx(nkt - 2)
                        emit_lctx(nkt - 1)
                        pending_epi[0] = (
                            lambda h=h, qg=qg, cb=cb, par=par, end=ct.n:
                            emit_epi(h, qg, cb, par, end))
                        qg_idx[0] += 1

                pending_epi[0]()
                pending_epi[0] = None
                while dq_live[0]:
                    dq_step()
                p.wait("g", cg)
                p.op("g", lambda e: e.collective_compute(
                    "AllGather", ALU.bypass, replica_groups=[list(range(8))],
                    ins=[gu_contrib.ap().opt()], outs=[wdeq_gu.ap().opt()]),
                    inc=cc, k=1)
                p.op("g", lambda e: e.collective_compute(
                    "AllGather", ALU.bypass, replica_groups=[list(range(8))],
                    ins=[dn_contrib.ap().opt()], outs=[wdeq_dn.ap().opt()]),
                    inc=cc, k=1)
                spbuf = [sb(ctxB, f"spbuf{i}", [128, HID], F32)
                         for i in range(2)]
                p.wait("t", ca)
                p.wait("t", cv)

                if debug_outputs:
                    p.barrier(allc)
                    p.op("a", lambda e: e.dma_start(
                        out=dbg["ctxT"][:, :],
                        in_=ctxT[:, :, :].rearrange("q h s -> q (h s)")),
                        inc=cdbg, k=16)

                sp_g = [gate(cg), gate(cg)]
                for qt32 in range(32):
                    sbi = qt32 % 2
                    for oc in range(4):
                        pi = oc
                        p.waits("t", psc_g[pi])
                        for ds in range(4):
                            p.mm(lambda e, ds=ds, qt=qt32, oc=oc, pi=pi:
                                 e.matmul(ps[pi][:, :],
                                          ctxT[:, ds, qt * 128:(qt + 1) * 128],
                                          wo_sb[:, ds, oc * 512:(oc + 1) * 512],
                                          start=(ds == 0), stop=(ds == 3)))
                        p.inc_last("t", ct)
                        p.wait("a", ct)
                        if oc == 0:
                            p.waits("a", sp_g[sbi])
                        p.op("a", lambda e, oc=oc, pi=pi, sbi=sbi: e.copy(
                            spbuf[sbi][:, oc * 512:(oc + 1) * 512],
                            ps[pi][:, :]), inc=ca, k=1)
                        psc_g[pi] = gate(ca)
                    p.wait("a", ca)
                    p.op("a", lambda e, qt=qt32, sbi=sbi: e.dma_start(
                        out=sp_in[qt * 128:(qt + 1) * 128, :],
                        in_=spbuf[sbi][:, :]), inc=c_sp[sbi], k=16)
                    sp_g[sbi] = gate(c_sp[sbi])

                p.wait("g", c_sp[0])
                p.wait("g", c_sp[1])
                p.op("g", lambda e: e.collective_compute(
                    "ReduceScatter", ALU.add, replica_groups=GROUPS,
                    ins=[sp_in.ap().opt()], outs=[s_own_d.ap().opt()]),
                    inc=cc, k=1)

                p.barrier(allc)
                with nc.named_scope("BW"), nc.Block() as blk:
                    p.flush(blk)

        with ExitStack() as ctxE:
            ident32_sb = sb(ctxE, "ident32_sb", [128, 128], F32)
            xld2 = [sb(ctxE, f"xld2_{i}", [128, HID], F32) for i in range(2)]
            p.op("y", lambda e: e.dma_start(out=ident32_sb[:, :],
                                            in_=ident32_in[:, :]), inc=cy, k=16)
            p.wait("t", cy)
            sq2 = sb(ctxE, "sq2", [128, HID], F32)
            xnbf2 = sb(ctxE, "xnbf2", [128, HID], BF16)
            xnTo = sb(ctxE, "xnTo", [128, 16, 1024], BF16)
            hT = sb(ctxE, "hT", [128, 44, 512], BF16)
            gslab = [sb(ctxE, f"gslab{i}", [128, 16, 128], BF16) for i in range(2)]
            dslab = [sb(ctxE, f"dslab{i}", [128, 44, 128], BF16) for i in range(2)]
            sg_sb = sb(ctxE, "sg_sb", [128, 512], BF16)
            ytp = sb(ctxE, "ytp", [128, 512], F32)
            ybuf = sb(ctxE, "ybuf", [128, 4, HID], F32)
            ynb = sb(ctxE, "ynb", [128, HID], F32)
            gbuf = sb(ctxE, "gbuf", [128, HID], F32)
            st_sb = sb(ctxE, "st_sb", [128, HID], F32)
            ot_sb = sb(ctxE, "ot_sb", [128, HID], F32)

            xld_gate = [[], []]
            xn_gate = []
            psTg = {0: [], 1: []}
            for gt in range(8):
                bi = gt % 2
                p.waits("y", xld_gate[bi])
                p.op("y", lambda e, b=xld2[bi], g=gt:
                     e.dma_start(out=b[:, :],
                                 in_=x_own[g * 128:(g + 1) * 128, :]),
                     inc=c_x2[bi], k=16)
                p.wait("v", c_x2[bi])
                p.op("v", lambda e, b=xld2[bi]: e.reduce_sum(
                    stat[:, 0:1], b[:, :], AX))
                p.wait("a", c_x2[bi])
                p.waits("a", xn_gate)
                p.op("a", lambda e, b=xld2[bi]: e.activation(
                    sq2[:, :], b[:, :], AF.Square, accum_out=stat[:, 7:8]),
                    inc=ca, k=1)
                p.drain("v")
                p.op("v", lambda e: e.tensor_scalar(
                    stat[:, 1:2], stat[:, 0:1], -1.0 / HID, None, ALU.mult))
                p.wait("v", ca)
                p.op("v", lambda e: e.tensor_scalar(
                    stat[:, 2:3], stat[:, 7:8], 1.0 / HID, None, ALU.mult))
                p.drain("v")
                p.op("v", lambda e: e.tensor_tensor(
                    stat[:, 3:4], stat[:, 1:2], stat[:, 1:2], ALU.mult))
                p.drain("v")
                p.op("v", lambda e: e.tensor_tensor(
                    stat[:, 4:5], stat[:, 2:3], stat[:, 3:4], ALU.subtract),
                    inc=cv, k=1)
                p.wait("a", cv)
                p.op("a", lambda e: e.activation(
                    stat[:, 5:6], stat[:, 4:5], AF.Sqrt, bias=stat[:, 12:13], scale=1.0),
                    inc=ca, k=1)
                p.wait("v", ca)
                p.op("v", lambda e: e.reciprocal(stat[:, 6:7], stat[:, 5:6]))
                p.drain("v")
                p.waits("v", xn_gate)
                p.op("v", lambda e, b=xld2[bi]: e.tensor_scalar(
                    xnbf2[:, :], b[:, :], stat[:, 1:2], stat[:, 6:7],
                    ALU.add, ALU.mult), inc=cv, k=1)
                xld_gate[bi] = gate(cv, ca)
                p.wait("t", cv)
                p.waits("t", psTg[0])
                p.waits("t", psTg[1])
                for i in range(16):
                    pst = psT if (i % 8) < 4 else psT2
                    sl = (i % 4) * 128
                    p.tr(lambda e, i=i, pst=pst, sl=sl: e.transpose(
                        pst[0:128, sl:sl + 128], xnbf2[:, i * 128:(i + 1) * 128],
                        ident_sb[:, :]))
                    if i % 8 == 7:
                        p.inc_last("t", ct)
                        p.wait("a", ct)
                        for i2 in range(i - 7, i + 1):
                            ps_ = psT if (i2 % 8) < 4 else psT2
                            sl2 = (i2 % 4) * 128
                            p.op("a", lambda e, i2=i2, ps_=ps_, sl2=sl2, g=gt:
                                 e.copy(xnTo[:, i2, g * 128:(g + 1) * 128],
                                        ps_[0:128, sl2:sl2 + 128]))
                        p.op("a", lambda e: e.drain(), inc=ca, k=1)
                        if i == 7:
                            p.wait("t", ca)
                xn_gate = gate(ca)
                psTg[0] = gate(ca)
                psTg[1] = gate(ca)
                if gt == 3:
                    ffn_xn_half = gate(ca)

            ffn_xn_gate = gate(ca)
            gsl_gate = [[], []]
            dsl_gate = [[], []]
            psg2 = {i: [] for i in range(6)}
            sg_gate = []
            hT_gate = []
            yt_gate = []
            ssq_gate = []
            ybuf_war = []

            for tg in range(2):
                for j in range(44):
                    for half, ot in ((0, j), (1, 44 + j)):
                        bi = (2 * j + half) % 2
                        qn = "y"
                        p.waits(qn, gsl_gate[bi])
                        p.wait(qn, cc, 4)
                        p.op(qn, lambda e, b=gslab[bi], o=ot: e.dma_start(
                            out=b[:, :, :], in_=wdeq_gu.ap()[o, :, :]
                            .rearrange("q (s o) -> q s o", s=16)), inc=c_gs[bi], k=16)
                        pi = half
                        p.waits("t", psg2[pi])
                        p.wait("t", c_gs[bi])
                        p.waits("t", ffn_xn_gate)
                        for su in range(16):
                            p.mm(lambda e, b=gslab[bi], su=su, tg=tg, pi=pi:
                                 e.matmul(ps[pi][:, :], b[:, su, :],
                                          xnTo[:, su, tg * 512:(tg + 1) * 512],
                                          start=(su == 0), stop=(su == 15)))
                        p.inc_last("t", ct)
                        gsl_gate[bi] = gate(ct)
                        if half == 0:
                            p.wait("a", ct)
                            p.waits("a", sg_gate)
                            p.op("a", lambda e, pi=pi: e.activation(
                                sg_sb[:, :], ps[pi][:, :], AF.Silu),
                                inc=ca, k=1)
                            psg2[pi] = gate(ca)
                        else:
                            p.wait("v", ct)
                            p.wait("v", ca)
                            p.op("v", lambda e, j=j, pi=pi: e.tensor_tensor(
                                hT[:, j, :], sg_sb[:, :], ps[pi][:, :],
                                ALU.mult), inc=cv, k=1)
                            p.drain("v")
                            sg_gate = gate(cv)
                            psg2[pi] = gate(cv)
                hT_gate = gate(cv)
                for ot in range(16):
                    bi = ot % 2
                    qn = "y"
                    p.waits(qn, dsl_gate[bi])
                    p.wait(qn, cc, 4)
                    p.op(qn, lambda e, b=dslab[bi], o=ot: e.dma_start(
                        out=b[:, :, :], in_=wdeq_dn.ap()[o, :, :]
                        .rearrange("q (s o) -> q s o", s=44)), inc=c_ds[bi], k=16)
                    pi = 4
                    p.waits("t", psg2[pi])
                    p.wait("t", c_ds[bi])
                    p.waits("t", hT_gate)
                    for su in range(44):
                        p.mm(lambda e, b=dslab[bi], su=su, pi=pi: e.matmul(
                            ps[pi][:, :], b[:, su, :], hT[:, su, :],
                            start=(su == 0), stop=(su == 43)))
                    p.inc_last("t", ct)
                    dsl_gate[bi] = gate(ct)
                    p.wait("a", ct)
                    p.waits("a", yt_gate)
                    p.op("a", lambda e, pi=pi: e.copy(ytp[:, :], ps[pi][:, :]),
                         inc=ca, k=1)
                    psg2[pi] = gate(ca)
                    p.wait("t", ca)
                    p.waits("t", psg2[5])
                    for tt in range(4):
                        p.tr(lambda e, tt=tt: e.transpose(
                            ps[5][:, tt * 128:(tt + 1) * 128],
                            ytp[:, tt * 128:(tt + 1) * 128], ident32_sb[:, :]))
                    p.inc_last("t", ct)
                    p.wait("a", ct)
                    p.waits("a", ybuf_war)
                    for tt in range(4):
                        p.op("a", lambda e, tt=tt, o=ot: e.copy(
                            ybuf[:, tt, o * 128:(o + 1) * 128],
                            ps[5][:, tt * 128:(tt + 1) * 128]))
                    p.op("a", lambda e: e.drain(), inc=ca, k=1)
                    yt_gate = gate(ca)
                    psg2[5] = gate(ca)
                p.wait("a", ca)
                p.waits("a", ssq_gate)
                for tt in range(4):
                    p.op("a", lambda e, tt=tt: e.activation(
                        sq2[:, :], ybuf[:, tt, :], AF.Square,
                        accum_out=stat[:, 24 + tt:25 + tt]))
                    p.drain("a")
                p.op("a", lambda e: e.activation(
                    stat[:, 32:36], stat[:, 24:28], AF.Sqrt, bias=stat[:, 13:14],
                    scale=1.0 / HID), inc=ca, k=1)
                p.wait("v", ca)
                p.op("v", lambda e: e.reciprocal(stat[:, 36:40], stat[:, 32:36]))
                p.drain("v")
                p.op("v", lambda e: e.engine_nop(), inc=cv, k=1)
                ssq_gate = gate(cv)
                ynb_gate = []
                gbuf_gate = []
                st_gate = []
                for tt in range(4):
                    gt = tg * 4 + tt
                    p.waits("v", ynb_gate)
                    p.op("v", lambda e, tt=tt: e.tensor_scalar(
                        ynb[:, :], ybuf[:, tt, :], stat[:, 36 + tt:37 + tt],
                        None, ALU.mult), inc=cv, k=1)
                    p.wait("a", cv)
                    p.waits("a", gbuf_gate)
                    p.op("a", lambda e: e.activation(
                        gbuf[:, :], ynb[:, :], AF.Silu), inc=ca, k=1)
                    ynb_gate = gate(ca)
                    bi = tt % 2
                    p.waits("y", xld_gate[bi])
                    p.op("y", lambda e, b=xld2[bi], g=gt: e.dma_start(
                        out=b[:, :], in_=x_own[g * 128:(g + 1) * 128, :]),
                        inc=c_x2[bi], k=16)
                    p.wait("y", cc)
                    p.waits("y", st_gate)
                    p.op("y", lambda e, g=gt: e.dma_start(
                        out=st_sb[:, :], in_=s_own_d[g * 128:(g + 1) * 128, :]),
                        inc=c_st, k=16)
                    p.wait("v", c_x2[bi]); p.wait("v", c_st)
                    p.wait("v", ca)
                    p.op("v", lambda e: e.tensor_tensor(
                        st_sb[:, :], st_sb[:, :], bo_sb[:, :], ALU.add))
                    p.drain("v")
                    p.op("v", lambda e: e.tensor_tensor(
                        st_sb[:, :], st_sb[:, :], gbuf[:, :], ALU.mult))
                    p.drain("v")
                    p.op("v", lambda e, b=xld2[bi]: e.tensor_tensor(
                        ot_sb[:, :], b[:, :], st_sb[:, :], ALU.add),
                        inc=cv, k=1)
                    xld_gate[bi] = gate(cv)
                    gbuf_gate = gate(cv)
                    st_gate = gate(cv)
                    p.wait("a", cv)
                    p.op("a", lambda e, g=gt: e.dma_start(
                        out=out_own[g * 128:(g + 1) * 128, :], in_=ot_sb[:, :]),
                        inc=cg, k=16)
                    p.wait("v", cg, cg.n)
                ssq_gate = gate(cv)
                ybuf_war = gate(cv)

            if debug_outputs:
                p.op("a", lambda e: e.dma_start(
                    out=dbg["sown"][:, :], in_=s_own_d[:, :]), inc=cdbg, k=16)

            p.wait("a", cg)
            p.barrier(allc)
            with nc.named_scope("E"), nc.Block() as blk:
                p.flush(blk)

    return nc



def _host_inputs(inputs):
    x = np.ascontiguousarray(np.asarray(inputs["L0_attn_x"], dtype=np.float32))
    wq_down = np.asarray(inputs["L0_attn_Wq_down"], np.float32)
    wq_up = np.asarray(inputs["L0_attn_Wq_up"], np.float32)
    wq_rope = np.asarray(inputs["L0_attn_Wq_rope"], np.float32)
    wkv_down = np.asarray(inputs["L0_attn_Wkv_down"], np.float32)
    wk_up = np.asarray(inputs["L0_attn_Wk_up"], np.float32)
    wk_rope = np.asarray(inputs["L0_attn_Wk_rope"], np.float32)
    wv_up = np.asarray(inputs["L0_attn_Wv_up"], np.float32)
    wo = np.asarray(inputs["L0_attn_Wo"], np.float32)
    bo = np.asarray(inputs["L0_attn_bo"], np.float32)
    gup = np.ascontiguousarray(np.asarray(inputs["L0_ffn_gate_up_packed"], np.int32))
    gus = np.asarray(inputs["L0_ffn_gate_up_scales"], np.float16)
    dnp = np.ascontiguousarray(np.asarray(inputs["L0_ffn_down_packed"], np.int32))
    dns = np.asarray(inputs["L0_ffn_down_scales"], np.float16)

    c = lambda a: np.ascontiguousarray(a)

    inv_freq = 1.0 / 10000.0 ** (np.arange(0, R, 2, dtype=np.float64) / R)
    t = np.arange(S, dtype=np.float64)
    freqs = t[:, None] * inv_freq[None, :]
    emb = np.concatenate([freqs, freqs], axis=-1)
    cos = np.cos(emb).astype(np.float32)
    sin = np.sin(emb).astype(np.float32)
    sin_adj = sin.copy()
    sin_adj[:, :16] = -sin_adj[:, :16]
    cos_t = c(np.tile(cos, (1, 4)))
    sin_t = c(np.tile(sin_adj, (1, 4)))
    ki = np.arange(128)[:, None]
    cc_ = np.arange(896)[None, :]
    mask = np.where((cc_ - 384) >= ki, 0.0, -10000.0).astype(np.float32)
    ident = np.eye(128, dtype=NPBF16)
    ident32 = np.eye(128, dtype=np.float32)
    bo_bc = c(np.tile(bo[None, :], (128, 1)))

    gus_T = np.ascontiguousarray(gus.T)
    dns_T = np.ascontiguousarray(dns.T)
    shared = {
        "wqd_t": c(wq_down.T), "wkvd_t": c(wkv_down.T),
        "wkrope_t": c(wk_rope.T), "bo_bc": bo_bc,
        "cos_t": cos_t, "sin_t": sin_t, "mask_in": mask,
        "ident_in": ident, "ident32_in": ident32,
    }
    maps = []
    for core in range(8):
        b, hg = core // 4, core % 4
        hs = slice(4 * hg * CDIM, (4 * hg + 4) * CDIM)
        rs = slice(4 * hg * R, (4 * hg + 4) * R)
        ds = slice(4 * hg * DH, (4 * hg + 4) * DH)
        m = dict(shared)
        m["xb_in"] = c(x[b])
        m["x_own"] = c(x[b, 1024 * hg:1024 * (hg + 1)])
        m["cos_own"] = c(cos_t[1024 * hg:1024 * (hg + 1)])
        m["sin_own"] = c(sin_t[1024 * hg:1024 * (hg + 1)])
        m["wqup_t"] = c(wq_up[hs].T)
        m["wkup_t"] = c(wk_up[hs].T)
        m["wqrope_t"] = c(wq_rope[rs].T)
        m["wvup_t"] = c(wv_up[ds].T)
        m["wo_t_own"] = c(wo.T[ds])
        m["gup_in"] = c(gup[core * 1408:(core + 1) * 1408])
        m["gus_t"] = c(gus_T[core * 1408:(core + 1) * 1408])
        m["dnp_in"] = c(dnp[core * 256:(core + 1) * 256])
        m["dns_t"] = c(dns_T[core * 256:(core + 1) * 256])
        maps.append(m)
    return maps


_NC_CACHE = {}


def kernel(**inputs):
    maps = _host_inputs(inputs)
    if "nc" not in _NC_CACHE:
        _NC_CACHE["nc"] = build_nc(debug_outputs=False)
    nc = _NC_CACHE["nc"]
    res = run_bass_kernel_spmd(nc, maps, core_ids=list(range(8)))
    out = np.zeros((B, S, HID), dtype=np.float32)
    for core in range(8):
        b, hg = core // 4, core % 4
        out[b, 1024 * hg:1024 * (hg + 1)] = res.results[core]["out_own"]
    return out
